# version 03
# baseline (speedup 1.0000x reference)
import numpy as np
import contextlib
import concourse.bass as bass
import concourse.mybir as mybir
from concourse.bass_utils import run_bass_kernel_spmd

F32 = mybir.dt.float32
BF16 = mybir.dt.bfloat16
U32 = mybir.dt.uint32
AF = mybir.ActivationFunctionType
ALU = mybir.AluOpType
AX = mybir.AxisListType.X

D = 4096
TOK = 1024
TT = 512
NTT = 2
KC = 32
EPS = 1e-5
C_EPS, C_NHALF, C_INVD, C_ZERO, C_ONE, C_ISQ = 0, 1, 2, 3, 4, 5
NKBL = [12, 16]
NKV = 8
GU_J = 22016
DN_J = 4096
DN_F = 86
GU_GRPS = [(g, 8 if g < 10 else 6) for g in range(11)]

STAGELIST = ["pro", "l0ffn", "attn0", "l0ffn", "l1ffn", "attn1", "l1ffn"]


def attn_g_sched():
    out = []
    P = 0
    for tq in range(NTT):
        for h in range(NKV):
            for pp in range(2):
                out.append((tq, h, pp, P, NKBL[tq]))
                P += 1
    return out


def diag_slot(tq, kb):
    if kb < 8:
        return kb
    if kb >= NKBL[tq] - 4:
        return 8 + (kb - (NKBL[tq] - 4))
    return None


class Eng:

    def __init__(self, eng, ctx, tag, prog_sem=None):
        self._e = eng
        self._ctx = ctx
        self._tag = tag
        self._s = prog_sem
        self._n = 0
        self._wcache = {}

    def wait_ge(self, semh, val):
        key = semh.num if hasattr(semh, "num") else id(semh)
        if self._wcache.get(key, -1) >= val:
            return
        self._wcache[key] = val
        self._e.wait_ge(semh, val)

    def mark(self, label):
        self._ctx.marks[f"{self._tag}:{self._ctx.si}:{label}"] = self._n

    def waitm(self, key):
        tag, label = key.split(":", 1)
        c = self._ctx.marks.get(f"{tag}:{self._ctx.si}:{label}", 0)
        if c:
            self.wait_ge(self._ctx.psems[tag], c)

    def __getattr__(self, name):
        f = getattr(self._e, name)
        if self._s is None:
            return f

        def wrap(*a, **k):
            if self._n:
                self.wait_ge(self._s, self._n)
            ins = f(*a, **k)
            ins.then_inc(self._s, 1)
            self._n += 1
            return ins

        return wrap


class Ctx:

    def __init__(self, nc, marks, cnt):
        self.nc = nc
        self.marks = marks
        self.cnt = cnt
        self.si = 0
        self.sems = {}
        self.psems = {}

    def I(self, ins, name, k=1):
        ins.then_inc(self.sems[name], k)
        key = (self.si, name)
        self.cnt[key] = self.cnt.get(key, 0) + k
        return ins

    def B(self, name):
        tot = 0
        for s in range(self.si):
            tot += self.marks.get(f"__b__{s}__{name}", 0)
        return tot

    def W(self, eng, name, val):
        v = self.B(name) + val
        if v > 0:
            eng.wait_ge(self.sems[name], v)


class Carver:
    def __init__(self, arena):
        self.ar = arena
        self.off = 0

    def take(self, rows, n_elems, dt):
        w = {F32: 2, U32: 2, BF16: 1}[dt]
        start = self.off
        self.off += n_elems * w
        v = self.ar[0:rows, start:start + n_elems * w]
        if dt != BF16:
            v = v.bitcast(dt)
        return v

    def mark(self):
        return self.off

    def reset(self, off):
        self.off = off


def _build(marks, cnt, stage_list, first_in, last_out):
    nc = bass.Bass("TRN2", target_bir_lowering=False, debug=False)
    ctx = Ctx(nc, marks, cnt)

    nx = len([k for k in stage_list if k != "pro"])
    xbufs = []
    for i in range(nx + 1):
        if i == 0:
            kind = "ExternalInput" if first_in else "Internal"
            xbufs.append(nc.dram_tensor("x_in", [D, TOK], F32, kind=kind).ap())
        elif i == nx:
            kind = "ExternalOutput" if last_out else "Internal"
            xbufs.append(nc.dram_tensor("x_out", [D, TOK], F32, kind=kind).ap())
        else:
            xbufs.append(nc.dram_tensor(f"x_mid{i}", [D, TOK], F32, kind="Internal").ap())

    dram = {}
    kinds = set(stage_list)
    CAST_WEIGHTS = []
    if "l0ffn" in kinds:
        CAST_WEIGHTS.append(("wbT", D))
        dram["bcol"] = nc.dram_tensor("bcol", [128, 32], F32, kind="ExternalInput").ap()
        dram["rtP"] = nc.dram_tensor("rtP", [128, KC * 8], F32, kind="ExternalInput").ap()
        dram["laP"] = nc.dram_tensor("laP", [128, KC * 128], F32, kind="ExternalInput").ap()
        dram["lbT"] = nc.dram_tensor("lbT", [128, D], F32, kind="ExternalInput").ap()
        dram["e8"] = nc.dram_tensor("e8", [8, 128], F32, kind="ExternalInput").ap()
        dram["idf"] = nc.dram_tensor("idf", [128, 128], F32, kind="ExternalInput").ap()
    if "attn0" in kinds or "attn1" in kinds:
        dram["csin"] = nc.dram_tensor("csin", [128, TOK], BF16, kind="ExternalInput").ap()
        dram["snin"] = nc.dram_tensor("snin", [128, TOK], BF16, kind="ExternalInput").ap()
        dram["mdin"] = nc.dram_tensor("mdin", [2 * 12 * 128, 512], BF16, kind="ExternalInput").ap()
        dram["bcin"] = nc.dram_tensor("bcin", [128, 32], F32, kind="ExternalInput").ap()
        dram["cc_in"] = nc.dram_tensor("cc_in", [2048, TOK], BF16, kind="Internal")
        dram["cc_out"] = nc.dram_tensor("cc_out", [4096, TOK], BF16, kind="Internal")
    for pre in ("a0", "a1"):
        if f"attn{pre[1]}" in kinds:
            CAST_WEIGHTS.append((f"{pre}_wqT", D))
            CAST_WEIGHTS.append((f"{pre}_wkT", 1024))
            CAST_WEIGHTS.append((f"{pre}_wvT", 1024))
            CAST_WEIGHTS.append((f"{pre}_woT", D))
    WGC_IDX = {wn: i + 1 for i, (wn, wd) in enumerate(CAST_WEIGHTS)}
    for (wn, wd) in CAST_WEIGHTS:
        dram[f"{wn}_sh"] = nc.dram_tensor(f"{wn}_sh", [512, wd], F32, kind="ExternalInput").ap()
        dram[f"{wn}_cast"] = nc.dram_tensor(f"{wn}_cast", [512, wd], BF16, kind="Internal").ap()
        dram[f"{wn}_bf"] = nc.dram_tensor(f"{wn}_bf", [D, wd], BF16, kind="Internal",
                                          addr_space="Shared").ap()
    if "attn0" in kinds:
        dram["ksin"] = nc.dram_tensor("ksin", [2 * 5 * 128, 512], F32, kind="ExternalInput").ap()
    if "l1ffn" in kinds:
        dram["qw_gu"] = nc.dram_tensor("qw_gu", [64, GU_J], U32, kind="ExternalInput").ap()
        dram["qz_gu"] = nc.dram_tensor("qz_gu", [4, 2752], U32, kind="ExternalInput").ap()
        dram["sc_gu"] = nc.dram_tensor("sc_gu", [4, GU_J], F32, kind="ExternalInput").ap()
        dram["qw_dn"] = nc.dram_tensor("qw_dn", [176, DN_J], U32, kind="ExternalInput").ap()
        dram["qz_dn"] = nc.dram_tensor("qz_dn", [11, 512], U32, kind="ExternalInput").ap()
        dram["sc_dn"] = nc.dram_tensor("sc_dn", [11, DN_J], F32, kind="ExternalInput").ap()
        dram["shin"] = nc.dram_tensor("shin", [128, 2752], U32, kind="ExternalInput").ap()
        dram["zscr_gu"] = nc.dram_tensor("zscr_gu", [4, GU_J], F32, kind="Internal").ap()
        dram["zscr_dn"] = nc.dram_tensor("zscr_dn", [11, DN_J], F32, kind="Internal").ap()
        dram["dq_gu"] = nc.dram_tensor("dq_gu", [512, GU_J], BF16, kind="Internal")
        dram["dq_dn"] = nc.dram_tensor("dq_dn", [1408, DN_J], BF16, kind="Internal")
        dram["wgu"] = nc.dram_tensor("wgu", [4096, GU_J], BF16, kind="Internal", addr_space="Shared")
        dram["wdn"] = nc.dram_tensor("wdn", [11264, DN_J], BF16, kind="Internal", addr_space="Shared")

    es = contextlib.ExitStack()
    with es:
        E = es.enter_context
        ARENA_N = 106000
        arena = E(nc.sbuf_tensor("arena", [128, ARENA_N], BF16))[:]
        cv = Carver(arena)
        S = {}
        S["xst"] = [cv.take(128, 512, F32) for _ in range(2)]
        S["sq"] = [cv.take(128, 512, F32) for _ in range(2)]
        S["wslab"] = [cv.take(128, 1024, BF16) for _ in range(4)]
        S["vslab"] = [cv.take(128, 512, BF16) for _ in range(2)]
        S["outst"] = cv.take(128, 4096, F32)
        S["cns"] = cv.take(128, 8, F32)
        S["onesf"] = cv.take(128, 1, F32)
        S["ones1"] = cv.take(1, 128, F32)
        S["onbf"] = cv.take(128, 128, BF16)
        S["idfb"] = cv.take(128, 128, F32)
        S["stat"] = cv.take(1, 3 * TT, F32)
        S["tmpf"] = cv.take(128, 512, F32)
        S["tmp2"] = cv.take(128, 512, F32)
        S["mbs"] = [cv.take(128, 512, F32) for _ in range(2)]
        S["rbs"] = [cv.take(128, 512, F32) for _ in range(2)]
        kind_base = cv.mark()

        KB = {}
        if "l0ffn" in kinds:
            cv.reset(kind_base)
            b = {}
            b["xbuf"] = cv.take(128, KC * TT, F32)
            b["lnb"] = cv.take(128, KC * TT, BF16)
            b["atb"] = cv.take(128, KC * 128, BF16)
            b["btb"] = cv.take(128, D, BF16)
            b["rtb"] = cv.take(128, KC * 8, BF16)
            b["bcolb"] = cv.take(128, 32, F32)
            b["e8b"] = cv.take(8, 128, F32)
            b["w8"] = cv.take(128, 32, F32)
            b["g8"] = cv.take(128, 8, F32)
            b["gcol"] = cv.take(128, 8, F32)
            b["wfm"] = cv.take(8, TT, F32)
            b["wexp"] = cv.take(128, TT, F32)
            b["hbf"] = cv.take(128, TT, BF16)
            KB["l0ffn"] = b
            assert cv.mark() <= ARENA_N, ("l0ffn", cv.mark())
        if "attn0" in kinds or "attn1" in kinds:
            cv.reset(kind_base)
            b = {}
            b["lnb"] = cv.take(128, KC * TT, BF16)
            b["qkv"] = cv.take(128, 16 * 1024, BF16)
            b["ksb"] = [cv.take(128, 2048, BF16) for _ in range(2)]
            b["vsb"] = [cv.take(128, 2048, BF16) for _ in range(2)]
            b["prb"] = [[cv.take(128, TT, BF16) for _ in range(2)] for _ in range(2)]
            b["qpre"] = cv.take(128, 8 * TT, BF16)
            b["qswp"] = cv.take(128, 8 * TT, BF16)
            b["csb"] = cv.take(128, TOK, BF16)
            b["snb"] = cv.take(128, TOK, BF16)
            b["mdb"] = cv.take(128, 24 * 512, BF16)
            b["bcb"] = cv.take(128, 32, F32)
            b["ksb_t"] = cv.take(128, 10 * 512, F32)
            b["pbc"] = cv.take(128, 2 * TT, F32)
            KB["attn"] = b
            assert cv.mark() <= ARENA_N, ("attn", cv.mark())
        if "l1ffn" in kinds:
            cv.reset(kind_base)
            b = {}
            b["lnb"] = cv.take(128, KC * TT, BF16)
            b["hbuf"] = cv.take(128, DN_F * TT, BF16)
            b["gsil"] = cv.take(128, 8 * TT, F32)
            b["zrow"] = cv.take(4, 2752, U32)
            b["zrowd"] = cv.take(11, 512, U32)
            KB["l1ffn"] = b
            assert cv.mark() <= ARENA_N, ("l1ffn", cv.mark())

        semnames = ["wgc", "wcc", "xl", "cw", "stq", "rtr", "bcm", "wtr", "wxm", "hps",
                    "mmc", "mmo", "kmm", "vmm", "qmm", "omm", "smm", "pcn",
                    "bcs", "ccd", "vmc", "swp", "sto", "gmm", "umm", "dmm",
                    "dqw", "zw", "dql", "ccw", "vp", "sp",
                    "wsl0", "wsl1", "wsl2", "wsl3", "xls0", "xls1",
                    "vsl0", "vsl1", "kvl0", "kvl1"]
        for name in semnames:
            ctx.sems[name] = E(nc.semaphore(name))
        ctx.psems = {"v": ctx.sems["vp"], "s": ctx.sems["sp"]}

        ps = [E(nc.psum_tensor(f"ps{i}", [128, TT], F32)) for i in range(8)]

        stages = []
        nxi = 0
        l1_seen = False
        for si, kind in enumerate(stage_list):
            if kind == "pro":
                stages.append(PRO(ctx, dram, S, (arena, kind_base), ps, CAST_WEIGHTS))
                continue
            if kind == "l0ffn":
                st_ = L0FFN(ctx, dram, S, KB["l0ffn"], ps, xbufs[nxi], xbufs[nxi + 1])
                st_.wgc_target = WGC_IDX["wbT"]
                stages.append(st_)
            elif kind in ("attn0", "attn1"):
                st_ = ATTN(ctx, dram, S, KB["attn"], ps, xbufs[nxi], xbufs[nxi + 1],
                           l0=(kind == "attn0"))
                st_.wgc_target = WGC_IDX[f"{st_.pre}_woT"]
                stages.append(st_)
            else:
                stages.append(L1FFN(ctx, dram, S, KB["l1ffn"], ps, xbufs[nxi], xbufs[nxi + 1],
                                    dequant=not l1_seen))
                l1_seen = True
            nxi += 1

        block = E(nc.Block())

        def run_all(eng, meth):
            for si, st in enumerate(stages):
                ctx.si = si
                ctx.W(eng, "sto", 0)
                getattr(st, meth)(eng)

        @block.sync
        def _(sy0):
            run_all(Eng(sy0, ctx, "y"), "sync")

        @block.gpsimd
        def _(g0):
            run_all(Eng(g0, ctx, "g"), "gpsimd")

        @block.scalar
        def _(sc0):
            run_all(Eng(sc0, ctx, "s", ctx.sems["sp"]), "scalar")

        @block.vector
        def _(v0):
            run_all(Eng(v0, ctx, "v", ctx.sems["vp"]), "vector")

        @block.tensor
        def _(t0):
            run_all(Eng(t0, ctx, "t"), "tensor")
    return nc


def build_module(stage_list=STAGELIST, first_in=True, last_out=True):
    marks = {}
    nc = None
    for _pass in range(2):
        cnt = {}
        nc = _build(marks, cnt, stage_list, first_in, last_out)
        for (si, name), v in cnt.items():
            marks[f"__b__{si}__{name}"] = v
    return nc


class L0FFN:
    def __init__(self, ctx, dram, S, b, ps, xin, xout):
        self.c = ctx
        self.d = dram
        self.S = S
        self.b = b
        self.ps = ps
        self.xin = xin
        self.xout = xout

    def sync(self, sy):
        c, S, d = self.c, self.S, self.d
        s = 0
        for tt in range(NTT):
            if tt > 0:
                sy.waitm("v:ep:0:3:7")
                c.W(sy, "stq", KC * tt)
            c.I(sy.dma_start(
                out=self.b["xbuf"].rearrange("p (cc t) -> p cc t", cc=KC),
                in_=self.xin.rearrange("(cc p) t -> p cc t", p=128)[:, :, tt * TT:(tt + 1) * TT],
            ), "xl", 16)
            for og in range(4):
                for kc in range(KC):
                    if s >= 4:
                        c.W(sy, "mmc", s - 3)
                    if s == 0:
                        sy.wait_ge(c.sems["wgc"], self.wgc_target)
                    c.I(sy.dma_start(
                        out=S["wslab"][s % 4],
                        in_=d["wbT_bf"][kc * 128:(kc + 1) * 128, og * 1024:(og + 1) * 1024],
                    ), f"wsl{s % 4}", 16)
                    s += 1
                go = tt * 4 + og
                sy.waitm(f"v:ep:{tt}:{og}:7")
                dst = self.xout[og * 1024:(og + 1) * 1024, tt * TT:(tt + 1) * TT]
                c.I(sy.dma_start(
                    out=dst.rearrange("(J p) t -> p J t", p=128),
                    in_=S["outst"].rearrange("p (J t) -> p J t", J=8),
                ), "sto", 16)
        c.W(sy, "sto", 16 * 8)

    def gpsimd(self, g):
        c, b, d = self.c, self.b, self.d
        c.I(g.dma_start(out=b["atb"], in_=d["laP"][:, :]), "cw", 16)
        c.I(g.dma_start(out=b["btb"], in_=d["lbT"][:, :]), "cw", 16)
        c.I(g.dma_start(out=b["rtb"], in_=d["rtP"][:, :]), "cw", 16)
        c.I(g.dma_start(out=b["bcolb"], in_=d["bcol"][:, :]), "cw", 16)
        c.I(g.dma_start(out=b["e8b"], in_=d["e8"][:, :]), "cw", 16)
        c.I(g.dma_start(out=self.S["idfb"], in_=d["idf"][:, :]), "cw", 16)

    def scalar(self, sc):
        c, S, b = self.c, self.S, self.b
        cns = S["cns"]
        sc.waitm("v:cst")
        for tt in range(NTT):
            c.W(sc, "xl", 16 * (tt + 1))
            for kc in range(KC):
                if tt * KC + kc >= 2:
                    c.W(sc, "stq", tt * KC + kc - 1)
                sc.activation(S["sq"][kc % 2], b["xbuf"][:, kc * TT:(kc + 1) * TT],
                              AF.Square, bias=cns[:, C_ZERO:C_ZERO + 1],
                              scale=cns[:, C_ONE:C_ONE + 1])
                sc.mark(f"sq:{tt}:{kc}")
            sc.waitm(f"v:var:{tt}")
            sc.activation(S["stat"][:, 2 * TT:3 * TT], S["stat"][:, TT:2 * TT],
                          AF.Ln, bias=cns[0:1, C_EPS:C_EPS + 1],
                          scale=cns[0:1, C_ONE:C_ONE + 1])
            sc.activation(S["stat"][:, 2 * TT:3 * TT], S["stat"][:, 2 * TT:3 * TT],
                          AF.Exp, bias=cns[0:1, C_ZERO:C_ZERO + 1],
                          scale=cns[0:1, C_NHALF:C_NHALF + 1])
            sc.mark(f"rstd:{tt}")
            for tc in range(4):
                sc.waitm(f"v:gmx:{tt}:{tc}")
                sc.activation(b["gcol"][:, 4:5], b["gcol"][:, 2:3], AF.Exp,
                              bias=cns[:, C_ZERO:C_ZERO + 1],
                              scale=cns[:, C_ONE:C_ONE + 1])
                sc.mark(f"gex:{tt}:{tc}")

    def vector(self, v):
        c, S, b, ps = self.c, self.S, self.b, self.ps
        cns = S["cns"]
        psA = ps[0][0:1, :]
        psB = ps[1][0:1, :]
        pr = ps[2][:, 0:8]
        pmb, prbc = ps[3], ps[4]
        ptr = ps[5][0:8, 0:128]
        pwx = ps[6]
        if c.si == 0:
            v.memset(cns[:, C_EPS:C_EPS + 1], EPS)
            v.memset(cns[:, C_NHALF:C_NHALF + 1], -0.5)
            v.memset(cns[:, C_INVD:C_INVD + 1], 1.0 / D)
            v.memset(cns[:, C_ZERO:C_ZERO + 1], 0.0)
            v.memset(cns[:, C_ONE:C_ONE + 1], 1.0)
            v.memset(cns[:, C_ISQ:C_ISQ + 1], 1.0 / float(np.sqrt(128.0)))
            v.memset(S["onesf"], 1.0)
            v.memset(S["ones1"], 1.0)
            v.memset(S["onbf"], 1.0)
        v.mark("cst")
        mb, rb = S["mbs"][0], S["rbs"][0]
        for tt in range(NTT):
            c.W(v, "stq", KC * (tt + 1))
            v.tensor_scalar_mul(S["stat"][:, 0:TT], psA, cns[0:1, C_INVD:C_INVD + 1])
            v.tensor_scalar_mul(S["stat"][:, TT:2 * TT], psB, cns[0:1, C_INVD:C_INVD + 1])
            v.tensor_mul(S["tmpf"][0:1, :], S["stat"][:, 0:TT], S["stat"][:, 0:TT])
            v.tensor_sub(S["stat"][:, TT:2 * TT], S["stat"][:, TT:2 * TT], S["tmpf"][0:1, :])
            v.mark(f"var:{tt}")
            c.W(v, "bcm", tt + 1)
            v.tensor_copy(mb, pmb[:])
            v.tensor_copy(rb, prbc[:])
            v.mark(f"bce:{tt}")
            for kc in range(KC):
                v.tensor_sub(S["tmpf"], b["xbuf"][:, kc * TT:(kc + 1) * TT], mb)
                v.tensor_mul(b["lnb"][:, kc * TT:(kc + 1) * TT], S["tmpf"], rb)
                v.mark(f"lnr:{tt}:{kc}")
            for tc in range(4):
                c.W(v, "rtr", tt * 4 + tc + 1)
                gcol, g8, w8 = b["gcol"], b["g8"], b["w8"]
                v.reduce_max(gcol[:, 0:1], pr, axis=AX)
                v.tensor_scalar(out=g8, in0=pr, scalar1=gcol[:, 0:1], scalar2=None,
                                op0=ALU.is_equal)
                w8s = w8[:, tc * 8:(tc + 1) * 8]
                v.scalar_tensor_tensor(out=w8s, in0=g8, scalar=-1e30, in1=pr,
                                       op0=ALU.mult, op1=ALU.add)
                v.reduce_max(gcol[:, 1:2], w8s, axis=AX)
                v.tensor_sub(gcol[:, 2:3], gcol[:, 1:2], gcol[:, 0:1])
                v.mark(f"gmx:{tt}:{tc}")
                v.waitm(f"s:gex:{tt}:{tc}")
                v.tensor_scalar_add(gcol[:, 5:6], gcol[:, 4:5], 1.0)
                v.reciprocal(gcol[:, 6:7], gcol[:, 5:6])
                v.tensor_mul(gcol[:, 7:8], gcol[:, 4:5], gcol[:, 6:7])
                v.tensor_scalar(out=g8, in0=g8, scalar1=gcol[:, 6:7], scalar2=None,
                                op0=ALU.mult)
                v.tensor_scalar(out=w8s, in0=pr, scalar1=gcol[:, 1:2], scalar2=gcol[:, 7:8],
                                op0=ALU.is_equal, op1=ALU.mult)
                v.tensor_add(w8s, w8s, g8)
                v.mark(f"wrd:{tt}:{tc}")
            for tc in range(4):
                c.W(v, "wtr", tt * 4 + tc + 1)
                v.tensor_copy(b["wfm"][:, tc * 128:(tc + 1) * 128], ptr)
                v.mark(f"wfe:{tt}:{tc}")
            c.W(v, "wxm", tt + 1)
            v.tensor_copy(b["wexp"], pwx[:])
            v.mark(f"wxe:{tt}")
            c.W(v, "hps", tt + 1)
            v.tensor_mul(b["hbf"], ps[7][:], b["wexp"])
            v.mark(f"hbd:{tt}")
            for og in range(4):
                go = tt * 4 + og
                c.W(v, "mmo", go * 8 + 8)
                if go >= 1:
                    c.W(v, "sto", 16 * go)
                for oc in range(8):
                    oco = og * 8 + oc
                    v.tensor_scalar(out=S["tmp2"], in0=ps[oc][:],
                                    scalar1=b["bcolb"][:, oco:oco + 1],
                                    scalar2=0.5, op0=ALU.add, op1=ALU.mult)
                    v.tensor_add(S["outst"][:, oc * TT:(oc + 1) * TT], S["tmp2"],
                                 b["xbuf"][:, oco * TT:(oco + 1) * TT])
                    v.mark(f"ep:{tt}:{og}:{oc}")

    def tensor(self, t):
        c, S, b, ps = self.c, self.S, self.b, self.ps
        psA = ps[0][0:1, :]
        psB = ps[1][0:1, :]
        pr = ps[2][:, 0:8]
        pmb, prbc = ps[3], ps[4]
        ptr = ps[5][0:8, 0:128]
        pwx = ps[6]
        c.W(t, "cw", 16 * 6)
        t.waitm("v:cst")
        s = 0
        for tt in range(NTT):
            c.W(t, "xl", 16 * (tt + 1))
            if tt > 0:
                t.waitm("v:ep:0:3:7")
            for kc in range(KC):
                t.matmul(psA, S["onesf"], b["xbuf"][:, kc * TT:(kc + 1) * TT],
                         start=(kc == 0), stop=(kc == KC - 1))
                t.waitm(f"s:sq:{tt}:{kc}")
                c.I(t.matmul(psB, S["onesf"], S["sq"][kc % 2],
                             start=(kc == 0), stop=(kc == KC - 1)), "stq", 1)
            t.waitm(f"s:rstd:{tt}")
            t.matmul(pmb[:], S["ones1"], S["stat"][:, 0:TT], start=True, stop=True)
            c.I(t.matmul(prbc[:], S["ones1"], S["stat"][:, 2 * TT:3 * TT],
                         start=True, stop=True), "bcm", 1)
            for tc in range(4):
                if tc > 0:
                    t.waitm(f"v:wrd:{tt}:{tc - 1}")
                elif tt > 0:
                    t.waitm("v:wrd:0:3")
                for kc in range(KC):
                    t.waitm(f"v:lnr:{tt}:{kc}")
                    mm = t.matmul(pr, b["lnb"][:, kc * TT + tc * 128:kc * TT + (tc + 1) * 128],
                                  b["rtb"][:, kc * 8:(kc + 1) * 8],
                                  start=(kc == 0), stop=(kc == KC - 1))
                c.I(mm, "rtr", 1)
            for kc in range(KC):
                t.waitm(f"v:lnr:{tt}:{kc}")
                mm = t.matmul(ps[7][:], b["atb"][:, kc * 128:(kc + 1) * 128],
                              b["lnb"][:, kc * TT:(kc + 1) * TT],
                              start=(kc == 0), stop=(kc == KC - 1))
            c.I(mm, "hps", 1)
            for tc in range(4):
                t.waitm(f"v:wrd:{tt}:{tc}")
                if tc > 0:
                    t.waitm(f"v:wfe:{tt}:{tc - 1}")
                elif tt > 0:
                    t.waitm("v:wfe:0:3")
                c.I(t.transpose(ptr, b["w8"][:, tc * 8:(tc + 1) * 8], S["idfb"]), "wtr", 1)
            t.waitm(f"v:wfe:{tt}:3")
            if tt > 0:
                t.waitm("v:wxe:0")
            c.I(t.matmul(pwx[:], b["e8b"], b["wfm"], start=True, stop=True), "wxm", 1)
            for og in range(4):
                go = tt * 4 + og
                if og == 0:
                    t.waitm(f"v:hbd:{tt}")
                for kc in range(KC):
                    c.W(t, f"wsl{s % 4}", 16 * (s // 4 + 1))
                    for jc in range(8):
                        if kc == 0 and og > 0:
                            t.waitm(f"v:ep:{tt}:{og - 1}:{jc}")
                        mm = t.matmul(
                            ps[jc][:],
                            S["wslab"][s % 4][:, jc * 128:(jc + 1) * 128],
                            b["lnb"][:, kc * TT:(kc + 1) * TT],
                            start=(kc == 0), stop=False,
                        )
                    c.I(mm, "mmc", 1)
                    s += 1
                t.waitm(f"v:hbd:{tt}")
                for jc in range(8):
                    oc = og * 8 + jc
                    c.I(t.matmul(ps[jc][:], b["btb"][:, oc * 128:(oc + 1) * 128], b["hbf"],
                                 start=False, stop=True), "mmo", 1)


class ATTN:
    def __init__(self, ctx, dram, S, b, ps, xin, xout, l0):
        self.c = ctx
        self.d = dram
        self.S = S
        self.b = b
        self.ps = ps
        self.xin = xin
        self.xout = xout
        self.l0 = l0
        self.pre = "a0" if l0 else "a1"
        self.SCHED = attn_g_sched()
        self.rope_phases = [("k", tt, 0) for tt in range(NTT)] + \
                           [("q", tq, og) for tq in range(NTT) for og in range(4)]
        self.rope_idx = {ph: i for i, ph in enumerate(self.rope_phases)}
        xp = []
        for tt in range(NTT):
            xp += [("st", tt), ("ln", tt)]
        for tq in range(NTT):
            xp += [("lq", tq), ("rs", tq)]
        self.xload_i = {}
        self.xload_rev = {}
        n = 0
        for pk in xp:
            for kc in range(KC):
                self.xload_i[(pk, kc)] = n
                self.xload_rev[n] = (pk, kc)
                n += 1
        self.smm_i = {}
        self.pcn_i = {}
        self.pcn_end = {}
        cs = cp = 0
        for (tq, h, pp, P, NKB) in self.SCHED:
            for kb in range(NKB):
                cs += 1
                self.smm_i[(P, kb)] = cs
            for kb in range(NKB):
                cp += 1
                self.pcn_i[(P, kb)] = cp
            self.pcn_end[P] = cp

    def _xgate(self, eng, gi):
        c = self.c
        if gi < 2:
            return
        pk, kc = self.xload_rev[gi - 2]
        kind, t = pk
        if kind == "st":
            eng.waitm(f"s:sq:{t}:{kc}")
            c.W(eng, "stq", t * KC + kc + 1)
        elif kind == "ln":
            eng.waitm(f"v:ln:{t}:{kc}")
        elif kind == "lq":
            eng.waitm(f"v:lq:{t}:{kc}")
        else:
            eng.waitm(f"v:ep:{t}:{kc}")

    def _xload(self, sy, pk, kc):
        c = self.c
        gi = self.xload_i[(pk, kc)]
        self._xgate(sy, gi)
        tt = pk[1]
        c.I(sy.dma_start(
            out=self.S["xst"][gi % 2],
            in_=self.xin.rearrange("(cc p) t -> p cc t", p=128)[:, kc, tt * TT:(tt + 1) * TT],
        ), f"xls{gi % 2}", 16)

    def sync(self, sy):
        c, S, b, d = self.c, self.S, self.b, self.d
        for tt in range(NTT):
            for kc in range(KC):
                self._xload(sy, ("st", tt), kc)
            for kc in range(KC):
                self._xload(sy, ("ln", tt), kc)
        for tt in range(NTT):
            for h in range(NKV):
                sy.waitm(f"v:krp:{tt}:{h}")
        for tt in range(NTT):
            for kvt in range(2):
                for tc in range(4):
                    sy.waitm(f"v:vev:{tt}:{kvt}:{tc}")
        kbuf = b["qkv"][:, 0:8 * 1024]
        vbuf = b["qkv"][:, 8 * 1024:16 * 1024]
        c.I(sy.dma_start(out=d["cc_in"].ap()[0:1024, :].rearrange("(h p) t -> p h t", p=128),
                         in_=kbuf.rearrange("p (h t) -> p h t", h=8)), "ccw", 16)
        c.I(sy.dma_start(out=d["cc_in"].ap()[1024:2048, :].rearrange("(cc p) m -> p cc m", p=128),
                         in_=vbuf.rearrange("p (cc m) -> p cc m", cc=8)), "ccw", 16)
        for tq in range(NTT):
            for kc in range(KC):
                self._xload(sy, ("lq", tq), kc)
            for h in range(NKV):
                ghi = tq * NKV + h
                if ghi == 0:
                    c.W(sy, "ccd", 1)
                if ghi >= 2:
                    Pdone = (ghi - 2) * 2 + 1
                    c.W(sy, "pcn", self.pcn_end[Pdone])
                bb = ghi % 2
                for half in range(2):
                    c.I(sy.dma_start(
                        out=b["ksb"][bb][:, half * 1024:(half + 1) * 1024],
                        in_=d["cc_out"].ap()[half * 2048 + h * 128:half * 2048 + (h + 1) * 128, :],
                    ), f"kvl{bb}", 16)
                    c.I(sy.dma_start(
                        out=b["vsb"][bb][:, half * 1024:(half + 1) * 1024].rearrange(
                            "p (cc m) -> p cc m", cc=8),
                        in_=d["cc_out"].ap()[half * 2048 + 1024:half * 2048 + 2048,
                                             h * 128:(h + 1) * 128].rearrange(
                            "(cc p) m -> p cc m", p=128),
                    ), f"kvl{bb}", 16)
            for og in range(4):
                for oc in range(8):
                    self._xload(sy, ("rs", tq), og * 8 + oc)
                go = tq * 4 + og
                sy.waitm(f"v:ep:{tq}:{og * 8 + 7}")
                dst = self.xout[og * 1024:(og + 1) * 1024, tq * TT:(tq + 1) * TT]
                c.I(sy.dma_start(
                    out=dst.rearrange("(J p) t -> p J t", p=128),
                    in_=S["outst"].rearrange("p (J t) -> p J t", J=8),
                ), "sto", 16)
        c.W(sy, "sto", 16 * 8)

    def gpsimd(self, g):
        c, S, b, d = self.c, self.S, self.b, self.d
        pre = self.pre
        nconst = 5 if self.l0 else 4
        c.I(g.dma_start(out=b["csb"], in_=d["csin"][:, :]), "cw", 16)
        c.I(g.dma_start(out=b["snb"], in_=d["snin"][:, :]), "cw", 16)
        c.I(g.dma_start(out=b["mdb"].rearrange("p (ss t) -> p ss t", ss=24),
                        in_=d["mdin"].rearrange("(ss p) t -> p ss t", p=128)), "cw", 16)
        c.I(g.dma_start(out=b["bcb"], in_=d["bcin"][:, :]), "cw", 16)
        if self.l0:
            c.I(g.dma_start(out=b["ksb_t"].rearrange("p (ss t) -> p ss t", ss=10),
                            in_=d["ksin"].rearrange("(ss p) t -> p ss t", p=128)), "cw", 16)
        self.nconst = nconst
        s = 0
        vs = 0

        g.wait_ge(c.sems["wgc"], self.wgc_target)

        def wload(kind, t_, og, kc):
            nonlocal s
            if s >= 4:
                c.W(g, "mmc", s - 3)
            if kind == "k":
                src = d[f"{pre}_wkT_bf"][kc * 128:(kc + 1) * 128, 0:1024]
            elif kind == "q":
                src = d[f"{pre}_wqT_bf"][kc * 128:(kc + 1) * 128, og * 1024:(og + 1) * 1024]
            else:
                src = d[f"{pre}_woT_bf"][kc * 128:(kc + 1) * 128, og * 1024:(og + 1) * 1024]
            c.I(g.dma_start(out=S["wslab"][s % 4], in_=src), f"wsl{s % 4}", 16)
            s += 1

        def vload(tt, kvt, kc):
            nonlocal vs
            if vs >= 2:
                c.W(g, "vmc", vs - 1)
            c.I(g.dma_start(out=S["vslab"][vs % 2],
                            in_=d[f"{pre}_wvT_bf"][kc * 128:(kc + 1) * 128,
                                                   kvt * 512:(kvt + 1) * 512]), f"vsl{vs % 2}", 16)
            vs += 1

        for tt in range(NTT):
            for kc in range(KC):
                wload("k", tt, 0, kc)
            for kvt in range(2):
                for kc in range(KC):
                    vload(tt, kvt, kc)
        c.W(g, "ccw", 32)
        c.I(g.collective_compute(
            "AllGather", mybir.AluOpType.bypass,
            replica_groups=[[0, 1], [2, 3], [4, 5], [6, 7]],
            ins=[d["cc_in"].ap().opt()],
            outs=[d["cc_out"].ap().opt()],
        ), "ccd", 1)
        for tq in range(NTT):
            for og in range(4):
                for kc in range(KC):
                    wload("q", tq, og, kc)
            for og in range(4):
                for kc in range(KC):
                    wload("o", tq, og, kc)

    def scalar(self, sc):
        c, S, b = self.c, self.S, self.b
        cns = S["cns"]
        ps = self.ps
        sc.waitm("v:cst")

        def swap_phase(ph):
            ri = self.rope_idx[ph]
            kind, t_, og = ph
            sc.waitm(f"v:evp:{kind}:{t_}:{og}")
            if ri > 0:
                pk, pt, pog = self.rope_phases[ri - 1]
                sc.waitm(f"v:rope:{pk}:{pt}:{pog}")
            c.I(sc._e.dma_start(out=b["qswp"][0:64, :], in_=b["qpre"][64:128, :]), "swp", 16)
            c.I(sc._e.dma_start(out=b["qswp"][64:128, :], in_=b["qpre"][0:64, :]), "swp", 16)

        for tt in range(NTT):
            for kc in range(KC):
                gi = self.xload_i[(("st", tt), kc)]
                c.W(sc, f"xls{gi % 2}", 16 * (gi // 2 + 1))
                if tt * KC + kc >= 2:
                    c.W(sc, "stq", tt * KC + kc - 1)
                sc.activation(S["sq"][kc % 2], S["xst"][gi % 2],
                              AF.Square, bias=cns[:, C_ZERO:C_ZERO + 1],
                              scale=cns[:, C_ONE:C_ONE + 1])
                sc.mark(f"sq:{tt}:{kc}")
            sc.waitm(f"v:var:{tt}")
            sc.activation(S["stat"][:, 2 * TT:3 * TT], S["stat"][:, TT:2 * TT],
                          AF.Ln, bias=cns[0:1, C_EPS:C_EPS + 1],
                          scale=cns[0:1, C_ONE:C_ONE + 1])
            sc.activation(S["stat"][:, 2 * TT:3 * TT], S["stat"][:, 2 * TT:3 * TT],
                          AF.Exp, bias=cns[0:1, C_ZERO:C_ZERO + 1],
                          scale=cns[0:1, C_NHALF:C_NHALF + 1])
            sc.mark(f"rstd:{tt}")
            swap_phase(("k", tt, 0))
        for tq in range(NTT):
            for og in range(4):
                swap_phase(("q", tq, og))
            for (tq2, h, pp, P, NKB) in self.SCHED:
                if tq2 != tq:
                    continue
                for kb in range(NKB):
                    c.W(sc, "smm", self.smm_i[(P, kb)])
                    if self.l0 and kb <= 4:
                        sc.waitm(f"v:ks:{P}:{kb}")
                    if self.pcn_i[(P, kb)] > 2:
                        c.W(sc, "pcn", self.pcn_i[(P, kb)] - 2)
                    bc = b["bcb"][:, tq * 16 + kb:tq * 16 + kb + 1]
                    sc.activation(b["prb"][0][kb % 2], ps[0 + kb % 2][:], AF.Exp,
                                  bias=bc, scale=cns[:, C_ISQ:C_ISQ + 1])
                    sc.activation(b["prb"][1][kb % 2], ps[2 + kb % 2][:], AF.Exp,
                                  bias=bc, scale=cns[:, C_ISQ:C_ISQ + 1])
                    sc.mark(f"ex:{P}:{kb}")

    def vector(self, v):
        c, S, b, ps = self.c, self.S, self.b, self.ps
        cns = S["cns"]
        psA = ps[0][0:1, :]
        psB = ps[1][0:1, :]
        pmb, prbc = ps[2], ps[3]
        kbuf = b["qkv"][:, 0:8 * 1024]
        vbuf = b["qkv"][:, 8 * 1024:16 * 1024]
        qtile = b["qkv"][:, 0:16 * 1024]
        rc = S["stat"][:, 0:2 * TT]
        v.mark("cst")

        def rope_evac(kind, t_, og):
            ri = self.rope_idx[(kind, t_, og)]
            c.W(v, "swp", 32 * ri)
            for j in range(8):
                if kind == "k":
                    if j < 7:
                        c.W(v, "kmm", t_ * 7 + j + 1)
                    else:
                        c.W(v, "mmc", t_ * 32 + 32)
                else:
                    if j < 7:
                        c.W(v, "qmm", (t_ * 4 + og) * 7 + j + 1)
                    else:
                        c.W(v, "mmc", 64 + t_ * 256 + og * 32 + 32)
                v.tensor_copy(b["qpre"][:, j * TT:(j + 1) * TT], ps[j][:])
                v.mark(f"evc:{kind}:{t_}:{og}:{j}")
            v.mark(f"evp:{kind}:{t_}:{og}")

        def rope_tt(kind, t_, og, dst_of_head):
            ri = self.rope_idx[(kind, t_, og)]
            c.W(v, "swp", 32 * (ri + 1))
            cs_t = b["csb"][:, t_ * TT:(t_ + 1) * TT]
            sn_t = b["snb"][:, t_ * TT:(t_ + 1) * TT]
            for j in range(8):
                v.tensor_mul(S["tmpf"], b["qpre"][:, j * TT:(j + 1) * TT], cs_t)
                v.tensor_mul(S["tmp2"], b["qswp"][:, j * TT:(j + 1) * TT], sn_t)
                v.tensor_add(dst_of_head(j), S["tmpf"], S["tmp2"])
                if kind == "k":
                    v.mark(f"krp:{t_}:{j}")
                else:
                    v.mark(f"qrp:{t_}:{og * 8 + j}")
            v.mark(f"rope:{kind}:{t_}:{og}")

        for tt in range(NTT):
            c.W(v, "stq", KC * (tt + 1))
            v.tensor_scalar_mul(S["stat"][:, 0:TT], psA, cns[0:1, C_INVD:C_INVD + 1])
            v.tensor_scalar_mul(S["stat"][:, TT:2 * TT], psB, cns[0:1, C_INVD:C_INVD + 1])
            v.tensor_mul(S["tmpf"][0:1, :], S["stat"][:, 0:TT], S["stat"][:, 0:TT])
            v.tensor_sub(S["stat"][:, TT:2 * TT], S["stat"][:, TT:2 * TT], S["tmpf"][0:1, :])
            v.mark(f"var:{tt}")
            c.W(v, "bcm", tt + 1)
            v.tensor_copy(S["mbs"][tt], pmb[:])
            v.tensor_copy(S["rbs"][tt], prbc[:])
            v.mark(f"bce:{tt}")
            for kc in range(KC):
                gi = self.xload_i[(("ln", tt), kc)]
                c.W(v, f"xls{gi % 2}", 16 * (gi // 2 + 1))
                v.tensor_sub(S["tmpf"], S["xst"][gi % 2], S["mbs"][tt])
                v.tensor_mul(b["lnb"][:, kc * TT:(kc + 1) * TT], S["tmpf"], S["rbs"][tt])
                v.mark(f"ln:{tt}:{kc}")
            rope_evac("k", tt, 0)
            for kvt in range(2):
                for tc in range(4):
                    if tc < 3:
                        c.W(v, "vmm", (tt * 2 + kvt) * 3 + tc + 1)
                    else:
                        c.W(v, "vmc", (tt * 2 + kvt) * 32 + 32)
                    v.tensor_copy(
                        vbuf[:, (tt * 4 + tc) * 1024 + kvt * 512:
                             (tt * 4 + tc) * 1024 + (kvt + 1) * 512],
                        ps[tc][:])
                    v.mark(f"vev:{tt}:{kvt}:{tc}")
            rope_tt("k", tt, 0,
                    lambda j, tt=tt: kbuf[:, j * 1024 + tt * TT:j * 1024 + (tt + 1) * TT])
        for tq in range(NTT):
            for kc in range(KC):
                gi = self.xload_i[(("lq", tq), kc)]
                c.W(v, f"xls{gi % 2}", 16 * (gi // 2 + 1))
                v.tensor_sub(S["tmpf"], S["xst"][gi % 2], S["mbs"][tq])
                v.tensor_mul(b["lnb"][:, kc * TT:(kc + 1) * TT], S["tmpf"], S["rbs"][tq])
                v.mark(f"lq:{tq}:{kc}")
            for og in range(4):
                rope_evac("q", tq, og)
                rope_tt("q", tq, og,
                        lambda j, og=og: qtile[:, (og * 8 + j) * TT:(og * 8 + j + 1) * TT])
            for (tq2, h, pp, P, NKB) in self.SCHED:
                if tq2 != tq:
                    continue
                g0 = h * 4 + pp * 2
                for kb in range(NKB):
                    if self.l0 and kb <= 4:
                        c.W(v, "smm", self.smm_i[(P, kb)])
                        kst = b["ksb_t"][:, (tq * 5 + kb) * 512:(tq * 5 + kb + 1) * 512]
                        v.tensor_mul(ps[0 + kb % 2][:], ps[0 + kb % 2][:], kst)
                        v.tensor_mul(ps[2 + kb % 2][:], ps[2 + kb % 2][:], kst)
                        v.mark(f"ks:{P}:{kb}")
                    slot = diag_slot(tq, kb)
                    if slot is not None:
                        v.waitm(f"s:ex:{P}:{kb}")
                        md = b["mdb"][:, (tq * 12 + slot) * 512:(tq * 12 + slot + 1) * 512]
                        v.tensor_mul(b["prb"][0][kb % 2], b["prb"][0][kb % 2], md)
                        v.tensor_mul(b["prb"][1][kb % 2], b["prb"][1][kb % 2], md)
                        v.mark(f"mk:{P}:{kb}")
                c.W(v, "pcn", self.pcn_end[P])
                v.reciprocal(rc[:, 0:TT], ps[6][0:1, :])
                v.reciprocal(rc[:, TT:2 * TT], ps[7][0:1, :])
                v.mark(f"rcp:{P}")
                c.W(v, "bcs", 2 * P + 2)
                v.tensor_copy(b["pbc"][:, 0:TT], ps[6][:])
                v.tensor_mul(b["lnb"][:, g0 * TT:(g0 + 1) * TT], ps[4][:], b["pbc"][:, 0:TT])
                v.mark(f"att:{P}:0")
                v.tensor_copy(b["pbc"][:, TT:2 * TT], ps[7][:])
                v.tensor_mul(b["lnb"][:, (g0 + 1) * TT:(g0 + 2) * TT], ps[5][:],
                             b["pbc"][:, TT:2 * TT])
                v.mark(f"att:{P}:1")
            for og in range(4):
                go = tq * 4 + og
                if go >= 1:
                    c.W(v, "sto", 16 * go)
                for oc in range(8):
                    oco = og * 8 + oc
                    gi = self.xload_i[(("rs", tq), oco)]
                    if oc < 7:
                        c.W(v, "omm", (tq * 4 + og) * 7 + oc + 1)
                    else:
                        c.W(v, "mmc", 64 + tq * 256 + 128 + og * 32 + 32)
                    c.W(v, f"xls{gi % 2}", 16 * (gi // 2 + 1))
                    v.tensor_add(S["outst"][:, oc * TT:(oc + 1) * TT],
                                 ps[oc][:], S["xst"][gi % 2])
                    v.mark(f"ep:{tq}:{oco}")

    def tensor(self, t):
        c, S, b, ps = self.c, self.S, self.b, self.ps
        psA = ps[0][0:1, :]
        psB = ps[1][0:1, :]
        pmb, prbc = ps[2], ps[3]
        qtile = b["qkv"][:, 0:16 * 1024]
        rc = S["stat"][:, 0:2 * TT]
        c.W(t, "cw", 16 * (5 if self.l0 else 4))
        t.waitm("v:cst")
        bank_free = {j: None for j in range(8)}

        def bwait(j):
            if bank_free[j]:
                t.waitm(bank_free[j])

        s_ctr = [0]
        vs_ctr = [0]
        for tt in range(NTT):
            for kc in range(KC):
                gi = self.xload_i[(("st", tt), kc)]
                c.W(t, f"xls{gi % 2}", 16 * (gi // 2 + 1))
                if kc == 0:
                    bwait(0)
                    bwait(1)
                t.matmul(psA, S["onesf"], S["xst"][gi % 2],
                         start=(kc == 0), stop=(kc == KC - 1))
                t.waitm(f"s:sq:{tt}:{kc}")
                c.I(t.matmul(psB, S["onesf"], S["sq"][kc % 2],
                             start=(kc == 0), stop=(kc == KC - 1)), "stq", 1)
            bank_free[0] = f"v:var:{tt}"
            bank_free[1] = f"v:var:{tt}"
            t.waitm(f"s:rstd:{tt}")
            bwait(2)
            bwait(3)
            t.matmul(pmb[:], S["ones1"], S["stat"][:, 0:TT], start=True, stop=True)
            c.I(t.matmul(prbc[:], S["ones1"], S["stat"][:, 2 * TT:3 * TT],
                         start=True, stop=True), "bcm", 1)
            bank_free[2] = f"v:bce:{tt}"
            bank_free[3] = f"v:bce:{tt}"
            for kc in range(KC):
                s = s_ctr[0]
                c.W(t, f"wsl{s % 4}", 16 * (s // 4 + 1))
                t.waitm(f"v:ln:{tt}:{kc}")
                for j in range(8):
                    if kc == 0:
                        bwait(j)
                    mm = t.matmul(ps[j][:], S["wslab"][s % 4][:, j * 128:(j + 1) * 128],
                                  b["lnb"][:, kc * TT:(kc + 1) * TT],
                                  start=(kc == 0), stop=(kc == KC - 1))
                    if j == 7:
                        c.I(mm, "mmc", 1)
                    elif kc == KC - 1:
                        c.I(mm, "kmm", 1)
                s_ctr[0] += 1
            for j in range(8):
                bank_free[j] = f"v:evc:k:{tt}:0:{j}"
            for kvt in range(2):
                for kc in range(KC):
                    vs = vs_ctr[0]
                    c.W(t, f"vsl{vs % 2}", 16 * (vs // 2 + 1))
                    for tc in range(4):
                        if kc == 0:
                            bwait(tc)
                        mm = t.matmul(ps[tc][:],
                                      b["lnb"][:, kc * TT + tc * 128:kc * TT + (tc + 1) * 128],
                                      S["vslab"][vs % 2],
                                      start=(kc == 0), stop=(kc == KC - 1))
                        if tc == 3:
                            c.I(mm, "vmc", 1)
                        elif kc == KC - 1:
                            c.I(mm, "vmm", 1)
                    vs_ctr[0] += 1
                for tc in range(4):
                    bank_free[tc] = f"v:vev:{tt}:{kvt}:{tc}"
        for tq in range(NTT):
            for og in range(4):
                for kc in range(KC):
                    s = s_ctr[0]
                    c.W(t, f"wsl{s % 4}", 16 * (s // 4 + 1))
                    t.waitm(f"v:lq:{tq}:{kc}")
                    for j in range(8):
                        if kc == 0:
                            bwait(j)
                        mm = t.matmul(ps[j][:], S["wslab"][s % 4][:, j * 128:(j + 1) * 128],
                                      b["lnb"][:, kc * TT:(kc + 1) * TT],
                                      start=(kc == 0), stop=(kc == KC - 1))
                        if j == 7:
                            c.I(mm, "mmc", 1)
                        elif kc == KC - 1:
                            c.I(mm, "qmm", 1)
                    s_ctr[0] += 1
                for j in range(8):
                    bank_free[j] = f"v:evc:q:{tq}:{og}:{j}"
            for (tq2, h, pp, P, NKB) in self.SCHED:
                if tq2 != tq:
                    continue
                ghi = tq * NKV + h
                g0 = h * 4 + pp * 2
                if pp == 0:
                    c.W(t, f"kvl{ghi % 2}", 16 * 4 * (ghi // 2 + 1))
                t.waitm(f"v:qrp:{tq}:{g0}")
                t.waitm(f"v:qrp:{tq}:{g0 + 1}")

                def consume(kb, P=P, NKB=NKB, tq=tq, ghi=ghi):
                    slot = diag_slot(tq, kb)
                    if slot is not None:
                        t.waitm(f"v:mk:{P}:{kb}")
                    else:
                        t.waitm(f"s:ex:{P}:{kb}")
                    st_ = (kb == 0)
                    sp_ = (kb == NKB - 1)
                    if st_:
                        bwait(6)
                        bwait(4)
                    t.matmul(ps[6][:], S["onbf"], b["prb"][0][kb % 2], start=st_, stop=sp_)
                    t.matmul(ps[4][:], b["vsb"][ghi % 2][:, kb * 128:(kb + 1) * 128],
                             b["prb"][0][kb % 2], start=st_, stop=sp_)
                    if st_:
                        bwait(7)
                        bwait(5)
                    t.matmul(ps[7][:], S["onbf"], b["prb"][1][kb % 2], start=st_, stop=sp_)
                    c.I(t.matmul(ps[5][:], b["vsb"][ghi % 2][:, kb * 128:(kb + 1) * 128],
                                 b["prb"][1][kb % 2], start=st_, stop=sp_), "pcn", 1)

                for kb in range(NKB):
                    bwait(0 + kb % 2)
                    bwait(2 + kb % 2)
                    t.matmul(ps[0 + kb % 2][:], b["ksb"][ghi % 2][:, kb * 128:(kb + 1) * 128],
                             qtile[:, g0 * TT:(g0 + 1) * TT], start=True, stop=True)
                    c.I(t.matmul(ps[2 + kb % 2][:],
                                 b["ksb"][ghi % 2][:, kb * 128:(kb + 1) * 128],
                                 qtile[:, (g0 + 1) * TT:(g0 + 2) * TT], start=True,
                                 stop=True), "smm", 1)
                    bank_free[0 + kb % 2] = f"s:ex:{P}:{kb}"
                    bank_free[2 + kb % 2] = f"s:ex:{P}:{kb}"
                    if kb >= 1:
                        consume(kb - 1)
                consume(NKB - 1)
                t.waitm(f"v:rcp:{P}")
                c.I(t.matmul(ps[6][:], S["ones1"], rc[:, 0:TT],
                             start=True, stop=True), "bcs", 1)
                c.I(t.matmul(ps[7][:], S["ones1"], rc[:, TT:2 * TT],
                             start=True, stop=True), "bcs", 1)
                bank_free[4] = f"v:att:{P}:0"
                bank_free[5] = f"v:att:{P}:1"
                bank_free[6] = f"v:att:{P}:0"
                bank_free[7] = f"v:att:{P}:1"
            for og in range(4):
                for kc in range(KC):
                    s = s_ctr[0]
                    c.W(t, f"wsl{s % 4}", 16 * (s // 4 + 1))
                    if og == 0:
                        Pc = (tq * NKV + kc // 4) * 2 + (kc % 4) // 2
                        t.waitm(f"v:att:{Pc}:{kc % 2}")
                    for j in range(8):
                        if kc == 0:
                            if og == 0:
                                bwait(j)
                            else:
                                t.waitm(f"v:ep:{tq}:{(og - 1) * 8 + j}")
                        mm = t.matmul(ps[j][:], S["wslab"][s % 4][:, j * 128:(j + 1) * 128],
                                      b["lnb"][:, kc * TT:(kc + 1) * TT],
                                      start=(kc == 0), stop=(kc == KC - 1))
                        if j == 7:
                            c.I(mm, "mmc", 1)
                        elif kc == KC - 1:
                            c.I(mm, "omm", 1)
                    s_ctr[0] += 1
                for j in range(8):
                    bank_free[j] = f"v:ep:{tq}:{og * 8 + j}"


class L1FFN:
    def __init__(self, ctx, dram, S, b, ps, xin, xout, dequant):
        self.c = ctx
        self.d = dram
        self.S = S
        self.b = b
        self.ps = ps
        self.xin = xin
        self.xout = xout
        self.dequant = dequant
        self.slabs = []
        for tt in range(NTT):
            for (grp, njc) in GU_GRPS:
                for kind in ("g", "u"):
                    for kc in range(KC):
                        self.slabs.append((tt, grp, kind, kc, njc))
            for og in range(4):
                for fc in range(DN_F):
                    self.slabs.append((tt, og, "d", fc, 8))
        self.dq_tiles = [("gu", g, jt, 2752) for g in range(4) for jt in range(8)] + \
                        [("dn", g, jt, 2048) for g in range(11) for jt in range(2)]
        xp = []
        for tt in range(NTT):
            xp += [("st", tt), ("ln", tt), ("rs", tt)]
        self.xload_i = {}
        self.xload_rev = {}
        n = 0
        for pk in xp:
            for kc in range(KC):
                self.xload_i[(pk, kc)] = n
                self.xload_rev[n] = (pk, kc)
                n += 1
        self.pos = {}
        self.gmm_c = {}
        self.umm_c = {}
        self.dmm_c = {}
        cg = cu = cd = 0
        for i, (a, bb, kind, cc, njc) in enumerate(self.slabs):
            self.pos[(a, bb, kind, cc)] = i
            if kind == "g" and cc == KC - 1:
                for j in range(njc - 1):
                    cg += 1
                    self.gmm_c[(a, bb, j)] = cg
            if kind == "u" and cc == KC - 1:
                for j in range(njc - 1):
                    cu += 1
                    self.umm_c[(a, bb, j)] = cu
            if kind == "d" and cc == DN_F - 1:
                for j in range(7):
                    cd += 1
                    self.dmm_c[(a, bb, j)] = cd
        hbuf = None

    def _views(self):
        hbuf = self.b["hbuf"]
        return {
            "nibu": hbuf[:, 0:5504].bitcast(U32),
            "zb": hbuf[:, 5504:11008].bitcast(F32),
            "sb": hbuf[:, 11008:16512].bitcast(F32),
            "nf": hbuf[:, 16512:22016].bitcast(F32),
            "ob": hbuf[:, 22016:24768],
            "shtile": hbuf[:, 24768:30272].bitcast(U32),
            "m15": hbuf[:, 30272:35776].bitcast(U32),
            "ztu": self.b["lnb"][:, 0:8192].bitcast(U32),
            "ztile": self.S["outst"][0:11, 0:DN_J],
        }

    def _xgate(self, eng, gi):
        c = self.c
        if gi < 2:
            return
        pk, kc = self.xload_rev[gi - 2]
        kind, t_ = pk
        if kind == "st":
            eng.waitm(f"s:sq:{t_}:{kc}")
            c.W(eng, "stq", t_ * KC + kc + 1)
        elif kind == "ln":
            eng.waitm(f"v:ln:{t_}:{kc}")
        else:
            eng.waitm(f"v:ep:{t_}:{kc}")

    def _xload(self, sy, pk, kc):
        c = self.c
        gi = self.xload_i[(pk, kc)]
        self._xgate(sy, gi)
        tt = pk[1]
        c.I(sy.dma_start(
            out=self.S["xst"][gi % 2],
            in_=self.xin.rearrange("(cc p) t -> p cc t", p=128)[:, kc, tt * TT:(tt + 1) * TT],
        ), f"xls{gi % 2}", 16)

    def sync(self, sy):
        c, S = self.c, self.S
        for tt in range(NTT):
            for kc in range(KC):
                self._xload(sy, ("st", tt), kc)
            for kc in range(KC):
                self._xload(sy, ("ln", tt), kc)
            for og in range(4):
                for oc in range(8):
                    self._xload(sy, ("rs", tt), og * 8 + oc)
                sy.waitm(f"v:ep:{tt}:{og * 8 + 7}")
                dst = self.xout[og * 1024:(og + 1) * 1024, tt * TT:(tt + 1) * TT]
                c.I(sy.dma_start(
                    out=dst.rearrange("(J p) t -> p J t", p=128),
                    in_=S["outst"].rearrange("p (J t) -> p J t", J=8),
                ), "sto", 16)
        c.W(sy, "sto", 16 * 8)

    def gpsimd(self, g):
        c, S, b, d = self.c, self.S, self.b, self.d
        V = self._views()
        if self.dequant:
            c.I(g.dma_start(out=b["zrow"], in_=d["qz_gu"][:, :]), "cw", 16)
            c.I(g.dma_start(out=b["zrowd"], in_=d["qz_dn"][:, :]), "cw", 16)
            c.I(g.dma_start(out=V["shtile"], in_=d["shin"][:, :]), "cw", 16)
            for jt in range(8):
                g.waitm(f"v:zgu:{jt}")
                c.I(g.dma_start(out=d["zscr_gu"][:, jt * 2752:(jt + 1) * 2752],
                                in_=V["ztile"][0:4, 0:2752]), "zw", 16)
            g.waitm("v:zdn")
            c.I(g.dma_start(out=d["zscr_dn"][:, :], in_=V["ztile"][0:11, :]), "zw", 16)
            for i, (kind, gg, jt, w) in enumerate(self.dq_tiles):
                if i >= 1:
                    c.W(g, "dqw", 16 * i)
                if kind == "gu":
                    qsrc = d["qw_gu"][gg * 16:(gg + 1) * 16, jt * w:(jt + 1) * w]
                    zsrc = d["zscr_gu"][gg:gg + 1, jt * w:(jt + 1) * w]
                    ssrc = d["sc_gu"][gg:gg + 1, jt * w:(jt + 1) * w]
                    dst = d["dq_gu"].ap()[gg * 128:(gg + 1) * 128, jt * w:(jt + 1) * w]
                else:
                    qsrc = d["qw_dn"][gg * 16:(gg + 1) * 16, jt * w:(jt + 1) * w]
                    zsrc = d["zscr_dn"][gg:gg + 1, jt * w:(jt + 1) * w]
                    ssrc = d["sc_dn"][gg:gg + 1, jt * w:(jt + 1) * w]
                    dst = d["dq_dn"].ap()[gg * 128:(gg + 1) * 128, jt * w:(jt + 1) * w]
                c.I(g.dma_start(out=V["nibu"][:, 0:w],
                                in_=qsrc.unsqueeze(1).broadcast_to([16, 8, w])), "dql", 16)
                c.W(g, "zw", 16 * 9)
                c.I(g.dma_start(out=V["zb"][:, 0:w], in_=zsrc.broadcast_to([128, w])), "dql", 16)
                c.I(g.dma_start(out=V["sb"][:, 0:w], in_=ssrc.broadcast_to([128, w])), "dql", 16)
                g.waitm(f"v:dq:{i}")
                c.I(g.dma_start(out=dst, in_=V["ob"][:, 0:w]), "dqw", 16)
            c.W(g, "dqw", 16 * len(self.dq_tiles))
            c.I(g.collective_compute(
                "AllGather", mybir.AluOpType.bypass,
                replica_groups=[[0, 1, 2, 3, 4, 5, 6, 7]],
                ins=[d["dq_gu"].ap().opt()], outs=[d["wgu"].ap().opt()],
            ), "ccd", 1)
            c.I(g.collective_compute(
                "AllGather", mybir.AluOpType.bypass,
                replica_groups=[[0, 1, 2, 3, 4, 5, 6, 7]],
                ins=[d["dq_dn"].ap().opt()], outs=[d["wdn"].ap().opt()],
            ), "ccd", 1)
            c.W(g, "ccd", 2)
        else:
            c.W(g, "ccd", 0)
        s = 0
        for (a, bb, kind, cc, njc) in self.slabs:
            if s >= 4:
                c.W(g, "mmc", s - 3)
            if kind == "g":
                src = d["wgu"].ap()[cc * 128:(cc + 1) * 128,
                                    bb * 1024:bb * 1024 + njc * 128]
            elif kind == "u":
                src = d["wgu"].ap()[cc * 128:(cc + 1) * 128,
                                    11008 + bb * 1024:11008 + bb * 1024 + njc * 128]
            else:
                src = d["wdn"].ap()[cc * 128:(cc + 1) * 128,
                                    bb * 1024:(bb + 1) * 1024]
            c.I(g.dma_start(out=S["wslab"][s % 4][:, 0:njc * 128], in_=src), f"wsl{s % 4}", 16)
            s += 1

    def scalar(self, sc):
        c, S, b, ps = self.c, self.S, self.b, self.ps
        cns = S["cns"]
        sc.waitm("v:cst")
        for tt in range(NTT):
            for kc in range(KC):
                gi = self.xload_i[(("st", tt), kc)]
                c.W(sc, f"xls{gi % 2}", 16 * (gi // 2 + 1))
                if tt * KC + kc >= 2:
                    c.W(sc, "stq", tt * KC + kc - 1)
                sc.activation(S["sq"][kc % 2], S["xst"][gi % 2],
                              AF.Square, bias=cns[:, C_ZERO:C_ZERO + 1],
                              scale=cns[:, C_ONE:C_ONE + 1])
                sc.mark(f"sq:{tt}:{kc}")
            sc.waitm(f"v:var:{tt}")
            sc.activation(S["stat"][:, 2 * TT:3 * TT], S["stat"][:, TT:2 * TT],
                          AF.Ln, bias=cns[0:1, C_EPS:C_EPS + 1],
                          scale=cns[0:1, C_ONE:C_ONE + 1])
            sc.activation(S["stat"][:, 2 * TT:3 * TT], S["stat"][:, 2 * TT:3 * TT],
                          AF.Exp, bias=cns[0:1, C_ZERO:C_ZERO + 1],
                          scale=cns[0:1, C_NHALF:C_NHALF + 1])
            sc.mark(f"rstd:{tt}")
            for (grp, njc) in GU_GRPS:
                for j in range(njc):
                    if j < njc - 1:
                        c.W(sc, "gmm", self.gmm_c[(tt, grp, j)])
                    else:
                        c.W(sc, "mmc", self.pos[(tt, grp, "g", KC - 1)] + 1)
                    sc.activation(b["gsil"][:, j * TT:(j + 1) * TT], ps[j][:], AF.Silu,
                                  bias=cns[:, C_ZERO:C_ZERO + 1],
                                  scale=cns[:, C_ONE:C_ONE + 1])
                    sc.mark(f"sil:{tt}:{grp}:{j}")

    def vector(self, v):
        c, S, b, ps = self.c, self.S, self.b, self.ps
        cns = S["cns"]
        V = self._views()
        psA = ps[0][0:1, :]
        psB = ps[1][0:1, :]
        pmb, prbc = ps[2], ps[3]
        v.mark("cst")
        if self.dequant:
            c.W(v, "cw", 48)
            for jt in range(8):
                if jt > 0:
                    c.W(v, "zw", 16 * jt)
                for s8 in range(8):
                    v.tensor_scalar(
                        out=V["ztu"][0:4, 0:2752].rearrange("p (a e) -> p a e", e=8)[:, :, s8],
                        in0=b["zrow"][0:4, jt * 344:(jt + 1) * 344],
                        scalar1=4 * s8, scalar2=15,
                        op0=ALU.logical_shift_right, op1=ALU.bitwise_and)
                v.tensor_scalar(out=V["ztile"][0:4, 0:2752], in0=V["ztu"][0:4, 0:2752],
                                scalar1=1, scalar2=None, op0=ALU.add)
                v.mark(f"zgu:{jt}")
            c.W(v, "zw", 16 * 8)
            for s8 in range(8):
                v.tensor_scalar(
                    out=V["ztu"][0:11, 0:4096].rearrange("p (a e) -> p a e", e=8)[:, :, s8],
                    in0=b["zrowd"][0:11, 0:512],
                    scalar1=4 * s8, scalar2=15,
                    op0=ALU.logical_shift_right, op1=ALU.bitwise_and)
            v.tensor_scalar(out=V["ztile"][0:11, :], in0=V["ztu"][0:11, 0:4096],
                            scalar1=1, scalar2=None, op0=ALU.add)
            v.mark("zdn")
            v.memset(V["m15"][:, :], 15)
            for i, (kind, gg, jt, w) in enumerate(self.dq_tiles):
                c.W(v, "dql", 48 * (i + 1))
                if i >= 1:
                    c.W(v, "dqw", 16 * i)
                v.tensor_tensor(out=V["nibu"][:, 0:w], in0=V["nibu"][:, 0:w],
                                in1=V["shtile"][:, 0:w], op=ALU.logical_shift_right)
                v.tensor_tensor(out=V["nibu"][:, 0:w], in0=V["nibu"][:, 0:w],
                                in1=V["m15"][:, 0:w], op=ALU.bitwise_and)
                v.tensor_copy(V["nf"][:, 0:w], V["nibu"][:, 0:w])
                v.tensor_sub(V["nf"][:, 0:w], V["nf"][:, 0:w], V["zb"][:, 0:w])
                v.tensor_mul(V["ob"][:, 0:w], V["nf"][:, 0:w], V["sb"][:, 0:w])
                v.mark(f"dq:{i}")
        for tt in range(NTT):
            c.W(v, "stq", KC * (tt + 1))
            v.tensor_scalar_mul(S["stat"][:, 0:TT], psA, cns[0:1, C_INVD:C_INVD + 1])
            v.tensor_scalar_mul(S["stat"][:, TT:2 * TT], psB, cns[0:1, C_INVD:C_INVD + 1])
            v.tensor_mul(S["tmpf"][0:1, :], S["stat"][:, 0:TT], S["stat"][:, 0:TT])
            v.tensor_sub(S["stat"][:, TT:2 * TT], S["stat"][:, TT:2 * TT], S["tmpf"][0:1, :])
            v.mark(f"var:{tt}")
            c.W(v, "bcm", tt + 1)
            v.tensor_copy(S["mbs"][tt], pmb[:])
            v.tensor_copy(S["rbs"][tt], prbc[:])
            v.mark(f"bce:{tt}")
            for kc in range(KC):
                gi = self.xload_i[(("ln", tt), kc)]
                c.W(v, f"xls{gi % 2}", 16 * (gi // 2 + 1))
                v.tensor_sub(S["tmpf"], S["xst"][gi % 2], S["mbs"][tt])
                v.tensor_mul(b["lnb"][:, kc * TT:(kc + 1) * TT], S["tmpf"], S["rbs"][tt])
                v.mark(f"ln:{tt}:{kc}")
            for (grp, njc) in GU_GRPS:
                for j in range(njc):
                    fc = grp * 8 + j
                    v.waitm(f"s:sil:{tt}:{grp}:{j}")
                    if j < njc - 1:
                        c.W(v, "umm", self.umm_c[(tt, grp, j)])
                    else:
                        c.W(v, "mmc", self.pos[(tt, grp, "u", KC - 1)] + 1)
                    v.tensor_mul(b["hbuf"][:, fc * TT:(fc + 1) * TT],
                                 b["gsil"][:, j * TT:(j + 1) * TT], ps[j][:])
                    v.mark(f"h:{tt}:{grp}:{j}")
            for og in range(4):
                go = tt * 4 + og
                if go >= 1:
                    c.W(v, "sto", 16 * go)
                for oc in range(8):
                    oco = og * 8 + oc
                    gi = self.xload_i[(("rs", tt), oco)]
                    if oc < 7:
                        c.W(v, "dmm", self.dmm_c[(tt, og, oc)])
                    else:
                        c.W(v, "mmc", self.pos[(tt, og, "d", DN_F - 1)] + 1)
                    c.W(v, f"xls{gi % 2}", 16 * (gi // 2 + 1))
                    v.scalar_tensor_tensor(out=S["outst"][:, oc * TT:(oc + 1) * TT],
                                           in0=ps[oc][:], scalar=0.5,
                                           in1=S["xst"][gi % 2],
                                           op0=ALU.mult, op1=ALU.add)
                    v.mark(f"ep:{tt}:{oco}")

    def tensor(self, t):
        c, S, b, ps = self.c, self.S, self.b, self.ps
        psA = ps[0][0:1, :]
        psB = ps[1][0:1, :]
        pmb, prbc = ps[2], ps[3]
        if self.dequant:
            c.W(t, "cw", 48)
        t.waitm("v:cst")
        bank_free = {j: None for j in range(8)}

        def bwait(j):
            if bank_free[j]:
                t.waitm(bank_free[j])

        s_ctr = [0]
        for tt in range(NTT):
            for kc in range(KC):
                gi = self.xload_i[(("st", tt), kc)]
                c.W(t, f"xls{gi % 2}", 16 * (gi // 2 + 1))
                if kc == 0:
                    bwait(0)
                    bwait(1)
                t.matmul(psA, S["onesf"], S["xst"][gi % 2],
                         start=(kc == 0), stop=(kc == KC - 1))
                t.waitm(f"s:sq:{tt}:{kc}")
                c.I(t.matmul(psB, S["onesf"], S["sq"][kc % 2],
                             start=(kc == 0), stop=(kc == KC - 1)), "stq", 1)
            bank_free[0] = f"v:var:{tt}"
            bank_free[1] = f"v:var:{tt}"
            t.waitm(f"s:rstd:{tt}")
            bwait(2)
            bwait(3)
            t.matmul(pmb[:], S["ones1"], S["stat"][:, 0:TT], start=True, stop=True)
            c.I(t.matmul(prbc[:], S["ones1"], S["stat"][:, 2 * TT:3 * TT],
                         start=True, stop=True), "bcm", 1)
            bank_free[2] = f"v:bce:{tt}"
            bank_free[3] = f"v:bce:{tt}"
            for (grp, njc) in GU_GRPS:
                for kc in range(KC):
                    s = s_ctr[0]
                    c.W(t, f"wsl{s % 4}", 16 * (s // 4 + 1))
                    t.waitm(f"v:ln:{tt}:{kc}")
                    for j in range(njc):
                        if kc == 0:
                            bwait(j)
                        mm = t.matmul(ps[j][:], S["wslab"][s % 4][:, j * 128:(j + 1) * 128],
                                      b["lnb"][:, kc * TT:(kc + 1) * TT],
                                      start=(kc == 0), stop=(kc == KC - 1))
                        if j == njc - 1:
                            c.I(mm, "mmc", 1)
                        elif kc == KC - 1:
                            c.I(mm, "gmm", 1)
                    s_ctr[0] += 1
                for kc in range(KC):
                    s = s_ctr[0]
                    c.W(t, f"wsl{s % 4}", 16 * (s // 4 + 1))
                    for j in range(njc):
                        if kc == 0:
                            t.waitm(f"s:sil:{tt}:{grp}:{j}")
                        mm = t.matmul(ps[j][:], S["wslab"][s % 4][:, j * 128:(j + 1) * 128],
                                      b["lnb"][:, kc * TT:(kc + 1) * TT],
                                      start=(kc == 0), stop=(kc == KC - 1))
                        if j == njc - 1:
                            c.I(mm, "mmc", 1)
                        elif kc == KC - 1:
                            c.I(mm, "umm", 1)
                    s_ctr[0] += 1
                for j in range(njc):
                    bank_free[j] = f"v:h:{tt}:{grp}:{j}"
            for og in range(4):
                for fc in range(DN_F):
                    s = s_ctr[0]
                    c.W(t, f"wsl{s % 4}", 16 * (s // 4 + 1))
                    if og == 0:
                        t.waitm(f"v:h:{tt}:{fc // 8}:{fc % 8}")
                    for j in range(8):
                        if fc == 0:
                            if og == 0:
                                bwait(j)
                            else:
                                t.waitm(f"v:ep:{tt}:{(og - 1) * 8 + j}")
                        mm = t.matmul(ps[j][:], S["wslab"][s % 4][:, j * 128:(j + 1) * 128],
                                      b["hbuf"][:, fc * TT:(fc + 1) * TT],
                                      start=(fc == 0), stop=(fc == DN_F - 1))
                        if j == 7:
                            c.I(mm, "mmc", 1)
                        elif fc == DN_F - 1:
                            c.I(mm, "dmm", 1)
                    s_ctr[0] += 1
                for j in range(8):
                    bank_free[j] = f"v:ep:{tt}:{og * 8 + j}"



def x_to_cores(x):
    out = []
    for cc in range(8):
        bb, h = cc // 2, cc % 2
        sl = np.asarray(x)[bb, h * 1024:(h + 1) * 1024, :]
        out.append(np.ascontiguousarray(sl.T))
    return out


def x_from_cores(shards):
    x = np.empty((4, 2048, D), np.float32)
    for cc in range(8):
        bb, h = cc // 2, cc % 2
        x[bb, h * 1024:(h + 1) * 1024, :] = shards[cc].T
    return x


def chunked_cols(a):
    dd, m = a.shape
    return np.ascontiguousarray(a.reshape(KC, 128, m).transpose(1, 0, 2).reshape(128, KC * m))


def rope_tables(q0):
    import ml_dtypes
    inv = 1.0 / (10000.0 ** (np.arange(0, 64, dtype=np.float64) / 64.0))
    pos = np.arange(q0, q0 + TOK, dtype=np.float64)
    ang = pos[None, :] * inv[:, None]
    cs = np.cos(ang)
    sn = np.sin(ang)
    cs2 = np.concatenate([cs, cs], axis=0)
    sn2 = np.concatenate([-sn, sn], axis=0)
    return (cs2.astype(ml_dtypes.bfloat16), sn2.astype(ml_dtypes.bfloat16))


def host_inputs(full, core):
    import ml_dtypes
    ins = {}
    e8 = np.zeros((8, 128), np.float32)
    for e in range(8):
        e8[e, e * 16:(e + 1) * 16] = 2.0
    r0 = core * 512
    ins["wbT_sh"] = np.ascontiguousarray(np.asarray(full["L0_ffn_W_base"]).T[r0:r0 + 512])
    ins["bcol"] = np.ascontiguousarray(np.asarray(full["L0_ffn_b_base"]).reshape(32, 128).T)
    ins["rtP"] = chunked_cols(np.ascontiguousarray(np.asarray(full["L0_ffn_W_router"]).T))
    ins["laP"] = chunked_cols(np.ascontiguousarray(
        np.asarray(full["L0_ffn_lora_A"]).reshape(128, D).T))
    ins["lbT"] = np.ascontiguousarray(
        np.asarray(full["L0_ffn_lora_B"]).transpose(0, 2, 1).reshape(128, D))
    ins["e8"] = e8
    ins["idf"] = np.eye(128, dtype=np.float32)
    q0 = (core % 2) * 1024
    cs2, sn2 = rope_tables(q0)
    ins["csin"], ins["snin"] = cs2, sn2
    md = np.zeros((2, 12, 128, 512), np.float32)
    bcols = np.full((2, 16), -20.0, np.float32)
    jj = np.arange(512)
    ii = np.arange(128)
    for tq in range(2):
        qbase = q0 + tq * 512
        for kb in range(16):
            if kb < NKBL[tq]:
                valid = (qbase + jj[None, :]) >= (kb * 128 + ii[:, None])
                if not valid.any():
                    bcols[tq, kb] = -1e9
                slot = diag_slot(tq, kb)
                if slot is not None:
                    md[tq, slot] = valid.astype(np.float32)
    ins["mdin"] = md.reshape(24 * 128, 512).astype(ml_dtypes.bfloat16)
    ins["bcin"] = np.broadcast_to(bcols.reshape(1, 32), (128, 32)).copy()
    ks = np.ones((2, 5, 128, 512), np.float32)
    kpos = np.arange(128)
    for tq in range(2):
        qbase = q0 + tq * 512
        qsel = (qbase + jj) >= 611
        for kb in range(5):
            kv = kb * 128 + kpos
            kscale = np.where(kv < 35, 0.5, np.where(kv < 611, 1.5, 1.0))
            ks[tq, kb] = np.where(qsel[None, :], kscale[:, None], 1.0)
    ins["ksin"] = ks.reshape(10 * 128, 512)
    ins["a0_wqT_sh"] = np.ascontiguousarray(np.asarray(full["L0_attn_wq"]).T[r0:r0 + 512].astype(np.float32))
    ins["a0_wkT_sh"] = np.ascontiguousarray(np.asarray(full["L0_attn_wk"]).T[r0:r0 + 512].astype(np.float32))
    ins["a0_wvT_sh"] = np.ascontiguousarray(np.asarray(full["L0_attn_wv"]).T[r0:r0 + 512].astype(np.float32))
    ins["a0_woT_sh"] = np.ascontiguousarray(np.asarray(full["L0_attn_wo"]).T[r0:r0 + 512].astype(np.float32))
    perm = np.empty(128, np.int64)
    perm[:64] = np.arange(0, 128, 2)
    perm[64:] = np.arange(1, 128, 2)

    def permcols(w):
        w = np.asarray(w)
        nh = w.shape[1] // 128
        return np.ascontiguousarray(
            w.reshape(w.shape[0], nh, 128)[:, :, perm].reshape(w.shape))
    ins["a1_wqT_sh"] = np.ascontiguousarray(permcols(full["L1_attn_Wq"])[r0:r0 + 512].astype(np.float32))
    ins["a1_wkT_sh"] = np.ascontiguousarray(permcols(full["L1_attn_Wk"])[r0:r0 + 512].astype(np.float32))
    ins["a1_wvT_sh"] = np.ascontiguousarray(np.asarray(full["L1_attn_Wv"])[r0:r0 + 512].astype(np.float32))
    ins["a1_woT_sh"] = np.ascontiguousarray(np.asarray(full["L1_attn_Wo"])[r0:r0 + 512].astype(np.float32))
    qw_gu = np.asarray(full["L1_ffn_qweight_gu"])
    qz_gu = np.asarray(full["L1_ffn_qzeros_gu"])
    sc_gu = np.asarray(full["L1_ffn_scales_gu"])
    qw_dn = np.asarray(full["L1_ffn_qweight_down"])
    qz_dn = np.asarray(full["L1_ffn_qzeros_down"])
    sc_dn = np.asarray(full["L1_ffn_scales_down"])
    g0 = core * 4
    d0 = core * 11
    qw_dn_pad = np.zeros((176 * 8, DN_J), np.uint32)
    qw_dn_pad[:qw_dn.shape[0]] = qw_dn
    qz_dn_pad = np.zeros((88, 512), np.uint32)
    qz_dn_pad[:qz_dn.shape[0]] = qz_dn
    sc_dn_pad = np.zeros((88, DN_J), np.float32)
    sc_dn_pad[:sc_dn.shape[0]] = sc_dn
    ins["qw_gu"] = np.ascontiguousarray(qw_gu[core * 64:(core + 1) * 64])
    ins["qz_gu"] = np.ascontiguousarray(qz_gu[g0:g0 + 4])
    ins["sc_gu"] = np.ascontiguousarray(sc_gu[g0:g0 + 4])
    ins["qw_dn"] = np.ascontiguousarray(qw_dn_pad[core * 176:(core + 1) * 176])
    ins["qz_dn"] = np.ascontiguousarray(qz_dn_pad[d0:d0 + 11])
    ins["sc_dn"] = np.ascontiguousarray(sc_dn_pad[d0:d0 + 11])
    ins["shin"] = np.broadcast_to(
        (((np.arange(128, dtype=np.uint32) % 8) * 4)[:, None]), (128, 2752)).copy()
    return ins


_NC = [None]


def kernel(**inputs):
    shards = x_to_cores(inputs["L0_attn_hidden_states"])
    if _NC[0] is None:
        _NC[0] = build_module()
    in_maps = [dict(host_inputs(inputs, cc), x_in=shards[cc]) for cc in range(8)]
    res = run_bass_kernel_spmd(_NC[0], in_maps, core_ids=list(range(8)))
    return x_from_cores([np.ascontiguousarray(res.results[cc]["x_out"]) for cc in range(8)])


class PRO:

    def __init__(self, ctx, dram, S, region, ps, cast_weights):
        self.c = ctx
        self.d = dram
        self.S = S
        self.ps = ps
        self.cw_list = cast_weights
        arena, kb0 = region
        o = [kb0]

        def take(rows, units):
            st = o[0]
            o[0] += units
            return arena[0:rows, st:st + units]

        self.nibu = take(128, 5504).bitcast(U32)
        self.zb = take(128, 5504).bitcast(F32)
        self.sb = take(128, 5504).bitcast(F32)
        self.nf = take(128, 5504).bitcast(F32)
        self.ob = take(128, 2752)
        self.shtile = take(128, 5504).bitcast(U32)
        self.m15 = take(128, 5504).bitcast(U32)
        self.ztu = take(128, 5504).bitcast(U32)
        self.ztile = take(11, 8192).bitcast(F32)
        self.zrow = take(4, 5504).bitcast(U32)
        self.zrowd = take(11, 1024).bitcast(U32)
        self.dq_tiles = [("gu", g, jt, 2752) for g in range(4) for jt in range(8)] + \
                        [("dn", g, jt, 2048) for g in range(11) for jt in range(2)]

    def sync(self, sy):
        pass

    def tensor(self, t):
        pass

    def gpsimd(self, g):
        c, d = self.c, self.d
        for i, (wn, wd) in enumerate(self.cw_list):
            c.I(g.dma_start(out=d[f"{wn}_cast"], in_=d[f"{wn}_sh"]), "wcc", 16)
        c.W(g, "wcc", 16 * len(self.cw_list))
        for i, (wn, wd) in enumerate(self.cw_list):
            c.I(g.collective_compute(
                "AllGather", mybir.AluOpType.bypass,
                replica_groups=[[0, 1, 2, 3, 4, 5, 6, 7]],
                ins=[d[f"{wn}_cast"].opt()], outs=[d[f"{wn}_bf"].opt()],
            ), "wgc", 1)

    def scalar(self, sc):
        return
        c, d = self.c, self.d
        if "dq_gu" not in d:
            return
        e = sc._e
        c.I(e.dma_start(out=self.zrow, in_=d["qz_gu"][:, :]), "cw", 16)
        c.I(e.dma_start(out=self.zrowd, in_=d["qz_dn"][:, :]), "cw", 16)
        c.I(e.dma_start(out=self.shtile, in_=d["shin"][:, :]), "cw", 16)
        for jt in range(8):
            sc.waitm(f"v:zgu:{jt}")
            c.I(e.dma_start(out=d["zscr_gu"][:, jt * 2752:(jt + 1) * 2752],
                            in_=self.ztile[0:4, 0:2752]), "zw", 16)
        sc.waitm("v:zdn")
        c.I(e.dma_start(out=d["zscr_dn"][:, :], in_=self.ztile[0:11, :]), "zw", 16)
        for i, (kind, gg, jt, w) in enumerate(self.dq_tiles):
            if i >= 1:
                c.W(sc, "dqw", 16 * i)
            if kind == "gu":
                qsrc = d["qw_gu"][gg * 16:(gg + 1) * 16, jt * w:(jt + 1) * w]
                zsrc = d["zscr_gu"][gg:gg + 1, jt * w:(jt + 1) * w]
                ssrc = d["sc_gu"][gg:gg + 1, jt * w:(jt + 1) * w]
                dst = d["dq_gu"].ap()[gg * 128:(gg + 1) * 128, jt * w:(jt + 1) * w]
            else:
                qsrc = d["qw_dn"][gg * 16:(gg + 1) * 16, jt * w:(jt + 1) * w]
                zsrc = d["zscr_dn"][gg:gg + 1, jt * w:(jt + 1) * w]
                ssrc = d["sc_dn"][gg:gg + 1, jt * w:(jt + 1) * w]
                dst = d["dq_dn"].ap()[gg * 128:(gg + 1) * 128, jt * w:(jt + 1) * w]
            c.I(e.dma_start(out=self.nibu[:, 0:w],
                            in_=qsrc.unsqueeze(1).broadcast_to([16, 8, w])), "dql", 16)
            c.W(sc, "zw", 16 * 9)
            c.I(e.dma_start(out=self.zb[:, 0:w], in_=zsrc.broadcast_to([128, w])), "dql", 16)
            c.I(e.dma_start(out=self.sb[:, 0:w], in_=ssrc.broadcast_to([128, w])), "dql", 16)
            sc.waitm(f"v:dq:{i}")
            c.I(e.dma_start(out=dst, in_=self.ob[:, 0:w]), "dqw", 16)

    def vector(self, v):
        c = self.c
        S = self.S
        cns = S["cns"]
        v.memset(cns[:, C_EPS:C_EPS + 1], EPS)
        v.memset(cns[:, C_NHALF:C_NHALF + 1], -0.5)
        v.memset(cns[:, C_INVD:C_INVD + 1], 1.0 / D)
        v.memset(cns[:, C_ZERO:C_ZERO + 1], 0.0)
        v.memset(cns[:, C_ONE:C_ONE + 1], 1.0)
        v.memset(cns[:, C_ISQ:C_ISQ + 1], 1.0 / float(np.sqrt(128.0)))
        v.memset(S["onesf"], 1.0)
        v.memset(S["ones1"], 1.0)
        v.memset(S["onbf"], 1.0)
        v.mark("cst")
        return
        c.W(v, "cw", 48)
        for jt in range(8):
            if jt > 0:
                c.W(v, "zw", 16 * jt)
            for s8 in range(8):
                v.tensor_scalar(
                    out=self.ztu[0:4, 0:2752].rearrange("p (a e) -> p a e", e=8)[:, :, s8],
                    in0=self.zrow[0:4, jt * 344:(jt + 1) * 344],
                    scalar1=4 * s8, scalar2=15,
                    op0=ALU.logical_shift_right, op1=ALU.bitwise_and)
            v.tensor_scalar(out=self.ztile[0:4, 0:2752], in0=self.ztu[0:4, 0:2752],
                            scalar1=1, scalar2=None, op0=ALU.add)
            v.mark(f"zgu:{jt}")
        c.W(v, "zw", 16 * 8)
        for half in range(2):
            for s8 in range(8):
                v.tensor_scalar(
                    out=self.ztu[0:11, 0:2048].rearrange("p (a e) -> p a e", e=8)[:, :, s8],
                    in0=self.zrowd[0:11, half * 256:(half + 1) * 256],
                    scalar1=4 * s8, scalar2=15,
                    op0=ALU.logical_shift_right, op1=ALU.bitwise_and)
            v.tensor_scalar(out=self.ztile[0:11, half * 2048:(half + 1) * 2048],
                            in0=self.ztu[0:11, 0:2048],
                            scalar1=1, scalar2=None, op0=ALU.add)
        v.mark("zdn")
        v.memset(self.m15[:, :], 15)
        for i, (kind, gg, jt, w) in enumerate(self.dq_tiles):
            c.W(v, "dql", 48 * (i + 1))
            if i >= 1:
                c.W(v, "dqw", 16 * i)
            v.tensor_tensor(out=self.nibu[:, 0:w], in0=self.nibu[:, 0:w],
                            in1=self.shtile[:, 0:w], op=ALU.logical_shift_right)
            v.tensor_tensor(out=self.nibu[:, 0:w], in0=self.nibu[:, 0:w],
                            in1=self.m15[:, 0:w], op=ALU.bitwise_and)
            v.tensor_copy(self.nf[:, 0:w], self.nibu[:, 0:w])
            v.tensor_sub(self.nf[:, 0:w], self.nf[:, 0:w], self.zb[:, 0:w])
            v.tensor_mul(self.ob[:, 0:w], self.nf[:, 0:w], self.sb[:, 0:w])
            v.mark(f"dq:{i}")


# version 04
# speedup vs baseline: 1.0481x; 1.0481x over previous
import numpy as np
import contextlib
import concourse.bass as bass
import concourse.mybir as mybir
from concourse.bass_utils import run_bass_kernel_spmd

F32 = mybir.dt.float32
BF16 = mybir.dt.bfloat16
U32 = mybir.dt.uint32
AF = mybir.ActivationFunctionType
ALU = mybir.AluOpType
AX = mybir.AxisListType.X

D = 4096
TOK = 1024
TT = 512
NTT = 2
KC = 32
EPS = 1e-5
C_EPS, C_NHALF, C_INVD, C_ZERO, C_ONE, C_ISQ = 0, 1, 2, 3, 4, 5
NKBL = [12, 16]
NKV = 8
GU_J = 22016
DN_J = 4096
DN_F = 86
GU_GRPS = [(g, 8 if g < 10 else 6) for g in range(11)]

STAGELIST = ["pro", "l0ffn", "attn0", "l0ffn", "l1ffn", "attn1", "l1ffn"]


def attn_g_sched():
    out = []
    P = 0
    for tq in range(NTT):
        for h in range(NKV):
            for pp in range(2):
                out.append((tq, h, pp, P, NKBL[tq]))
                P += 1
    return out


def diag_slot(tq, kb):
    if kb < 8:
        return kb
    if kb >= NKBL[tq] - 4:
        return 8 + (kb - (NKBL[tq] - 4))
    return None


class Eng:

    def __init__(self, eng, ctx, tag, prog_sem=None):
        self._e = eng
        self._ctx = ctx
        self._tag = tag
        self._s = prog_sem
        self._n = 0
        self._wcache = {}

    def wait_ge(self, semh, val):
        key = semh.num if hasattr(semh, "num") else id(semh)
        if self._wcache.get(key, -1) >= val:
            return
        self._wcache[key] = val
        self._e.wait_ge(semh, val)

    def mark(self, label):
        self._ctx.marks[f"{self._tag}:{self._ctx.si}:{label}"] = self._n

    def waitm(self, key):
        tag, label = key.split(":", 1)
        c = self._ctx.marks.get(f"{tag}:{self._ctx.si}:{label}", 0)
        if c:
            self.wait_ge(self._ctx.psems[tag], c)

    def __getattr__(self, name):
        f = getattr(self._e, name)
        if self._s is None:
            return f

        def wrap(*a, **k):
            if self._n:
                self.wait_ge(self._s, self._n)
            ins = f(*a, **k)
            ins.then_inc(self._s, 1)
            self._n += 1
            return ins

        return wrap


class Ctx:

    def __init__(self, nc, marks, cnt):
        self.nc = nc
        self.marks = marks
        self.cnt = cnt
        self.si = 0
        self.sems = {}
        self.psems = {}

    def I(self, ins, name, k=1):
        ins.then_inc(self.sems[name], k)
        key = (self.si, name)
        self.cnt[key] = self.cnt.get(key, 0) + k
        return ins

    def B(self, name):
        tot = 0
        for s in range(self.si):
            tot += self.marks.get(f"__b__{s}__{name}", 0)
        return tot

    def W(self, eng, name, val):
        v = self.B(name) + val
        if v > 0:
            eng.wait_ge(self.sems[name], v)


class Carver:
    def __init__(self, arena):
        self.ar = arena
        self.off = 0

    def take(self, rows, n_elems, dt):
        w = {F32: 2, U32: 2, BF16: 1}[dt]
        start = self.off
        self.off += n_elems * w
        v = self.ar[0:rows, start:start + n_elems * w]
        if dt != BF16:
            v = v.bitcast(dt)
        return v

    def mark(self):
        return self.off

    def reset(self, off):
        self.off = off


def _build(marks, cnt, stage_list, first_in, last_out):
    nc = bass.Bass("TRN2", target_bir_lowering=False, debug=False)
    ctx = Ctx(nc, marks, cnt)

    nx = len([k for k in stage_list if k != "pro"])
    xbufs = []
    for i in range(nx + 1):
        if i == 0:
            kind = "ExternalInput" if first_in else "Internal"
            xbufs.append(nc.dram_tensor("x_in", [D, TOK], F32, kind=kind).ap())
        elif i == nx:
            kind = "ExternalOutput" if last_out else "Internal"
            xbufs.append(nc.dram_tensor("x_out", [D, TOK], F32, kind=kind).ap())
        else:
            xbufs.append(nc.dram_tensor(f"x_mid{i}", [D, TOK], F32, kind="Internal").ap())

    dram = {}
    kinds = set(stage_list)
    CAST_WEIGHTS = []
    if "l0ffn" in kinds:
        CAST_WEIGHTS.append(("wbT", D))
        dram["bcol"] = nc.dram_tensor("bcol", [128, 32], F32, kind="ExternalInput").ap()
        dram["rtP"] = nc.dram_tensor("rtP", [128, KC * 8], F32, kind="ExternalInput").ap()
        dram["laP"] = nc.dram_tensor("laP", [128, KC * 128], F32, kind="ExternalInput").ap()
        dram["lbT"] = nc.dram_tensor("lbT", [128, D], F32, kind="ExternalInput").ap()
        dram["e8"] = nc.dram_tensor("e8", [8, 128], F32, kind="ExternalInput").ap()
        dram["idf"] = nc.dram_tensor("idf", [128, 128], F32, kind="ExternalInput").ap()
    if "attn0" in kinds or "attn1" in kinds:
        dram["csin"] = nc.dram_tensor("csin", [128, TOK], BF16, kind="ExternalInput").ap()
        dram["snin"] = nc.dram_tensor("snin", [128, TOK], BF16, kind="ExternalInput").ap()
        dram["mdin"] = nc.dram_tensor("mdin", [2 * 12 * 128, 512], BF16, kind="ExternalInput").ap()
        dram["bcin"] = nc.dram_tensor("bcin", [128, 32], F32, kind="ExternalInput").ap()
        dram["cc_in"] = nc.dram_tensor("cc_in", [2048, TOK], BF16, kind="Internal")
        dram["cc_out"] = nc.dram_tensor("cc_out", [4096, TOK], BF16, kind="Internal")
    for pre in ("a0", "a1"):
        if f"attn{pre[1]}" in kinds:
            CAST_WEIGHTS.append((f"{pre}_wqT", D))
            CAST_WEIGHTS.append((f"{pre}_wkT", 1024))
            CAST_WEIGHTS.append((f"{pre}_wvT", 1024))
            CAST_WEIGHTS.append((f"{pre}_woT", D))
    WGC_IDX = {wn: i + 1 for i, (wn, wd) in enumerate(CAST_WEIGHTS)}
    for (wn, wd) in CAST_WEIGHTS:
        dram[f"{wn}_sh"] = nc.dram_tensor(f"{wn}_sh", [512, wd], F32, kind="ExternalInput").ap()
        dram[f"{wn}_cast"] = nc.dram_tensor(f"{wn}_cast", [512, wd], BF16, kind="Internal").ap()
        dram[f"{wn}_bf"] = nc.dram_tensor(f"{wn}_bf", [D, wd], BF16, kind="Internal",
                                          addr_space="Shared").ap()
    if "attn0" in kinds:
        dram["ksin"] = nc.dram_tensor("ksin", [2 * 5 * 128, 512], F32, kind="ExternalInput").ap()
    if "l1ffn" in kinds:
        dram["qw_gu"] = nc.dram_tensor("qw_gu", [64, GU_J], U32, kind="ExternalInput").ap()
        dram["qz_gu"] = nc.dram_tensor("qz_gu", [4, 2752], U32, kind="ExternalInput").ap()
        dram["sc_gu"] = nc.dram_tensor("sc_gu", [4, GU_J], F32, kind="ExternalInput").ap()
        dram["qw_dn"] = nc.dram_tensor("qw_dn", [176, DN_J], U32, kind="ExternalInput").ap()
        dram["qz_dn"] = nc.dram_tensor("qz_dn", [11, 512], U32, kind="ExternalInput").ap()
        dram["sc_dn"] = nc.dram_tensor("sc_dn", [11, DN_J], F32, kind="ExternalInput").ap()
        dram["shin"] = nc.dram_tensor("shin", [128, 2752], U32, kind="ExternalInput").ap()
        dram["zscr_gu"] = nc.dram_tensor("zscr_gu", [4, GU_J], F32, kind="Internal").ap()
        dram["zscr_dn"] = nc.dram_tensor("zscr_dn", [11, DN_J], F32, kind="Internal").ap()
        dram["dq_gu"] = nc.dram_tensor("dq_gu", [512, GU_J], BF16, kind="Internal")
        dram["dq_dn"] = nc.dram_tensor("dq_dn", [1408, DN_J], BF16, kind="Internal")
        dram["wgu"] = nc.dram_tensor("wgu", [4096, GU_J], BF16, kind="Internal", addr_space="Shared")
        dram["wdn"] = nc.dram_tensor("wdn", [11264, DN_J], BF16, kind="Internal", addr_space="Shared")

    es = contextlib.ExitStack()
    with es:
        E = es.enter_context
        ARENA_N = 106000
        arena = E(nc.sbuf_tensor("arena", [128, ARENA_N], BF16))[:]
        cv = Carver(arena)
        S = {}
        S["xst"] = [cv.take(128, 512, F32) for _ in range(2)]
        S["sq"] = [cv.take(128, 512, F32) for _ in range(2)]
        S["wslab"] = [cv.take(128, 1024, BF16) for _ in range(4)]
        S["vslab"] = [cv.take(128, 512, BF16) for _ in range(2)]
        S["outst"] = cv.take(128, 4096, F32)
        S["cns"] = cv.take(128, 8, F32)
        S["onesf"] = cv.take(128, 1, F32)
        S["ones1"] = cv.take(1, 128, F32)
        S["onbf"] = cv.take(128, 128, BF16)
        S["idfb"] = cv.take(128, 128, F32)
        S["stat"] = cv.take(1, 3 * TT, F32)
        S["tmpf"] = cv.take(128, 512, F32)
        S["tmp2"] = cv.take(128, 512, F32)
        S["mbs"] = [cv.take(128, 512, F32) for _ in range(2)]
        S["rbs"] = [cv.take(128, 512, F32) for _ in range(2)]
        kind_base = cv.mark()

        KB = {}
        if "l0ffn" in kinds:
            cv.reset(kind_base)
            b = {}
            b["xbuf"] = cv.take(128, KC * TT, F32)
            b["lnb"] = cv.take(128, KC * TT, BF16)
            b["atb"] = cv.take(128, KC * 128, BF16)
            b["btb"] = cv.take(128, D, BF16)
            b["rtb"] = cv.take(128, KC * 8, BF16)
            b["bcolb"] = cv.take(128, 32, F32)
            b["e8b"] = cv.take(8, 128, F32)
            b["w8"] = cv.take(128, 32, F32)
            b["g8"] = cv.take(128, 8, F32)
            b["gcol"] = cv.take(128, 8, F32)
            b["wfm"] = cv.take(8, TT, F32)
            b["wexp"] = cv.take(128, TT, F32)
            b["hbf"] = cv.take(128, TT, BF16)
            KB["l0ffn"] = b
            l0ffn_carve_end = cv.mark()
            assert cv.mark() <= ARENA_N, ("l0ffn", cv.mark())
        if "attn0" in kinds or "attn1" in kinds:
            cv.reset(kind_base)
            b = {}
            b["lnb"] = cv.take(128, KC * TT, BF16)
            b["qkv"] = cv.take(128, 16 * 1024, BF16)
            b["ksb"] = [cv.take(128, 2048, BF16) for _ in range(2)]
            b["vsb"] = [cv.take(128, 2048, BF16) for _ in range(2)]
            b["prb"] = [[cv.take(128, TT, BF16) for _ in range(2)] for _ in range(2)]
            b["qpre"] = cv.take(128, 8 * TT, BF16)
            b["qswp"] = cv.take(128, 8 * TT, BF16)
            b["csb"] = cv.take(128, TOK, BF16)
            b["snb"] = cv.take(128, TOK, BF16)
            b["mdb"] = cv.take(128, 24 * 512, BF16)
            b["bcb"] = cv.take(128, 32, F32)
            b["ksb_t"] = cv.take(128, 10 * 512, F32)
            b["pbc"] = cv.take(128, 2 * TT, F32)
            KB["attn"] = b
            assert cv.mark() <= ARENA_N, ("attn", cv.mark())
        if "l1ffn" in kinds:
            cv.reset(kind_base)
            b = {}
            b["lnb"] = cv.take(128, KC * TT, BF16)
            b["hbuf"] = cv.take(128, DN_F * TT, BF16)
            b["gsil"] = cv.take(128, 8 * TT, F32)
            b["zrow"] = cv.take(4, 2752, U32)
            b["zrowd"] = cv.take(11, 512, U32)
            KB["l1ffn"] = b
            assert cv.mark() <= ARENA_N, ("l1ffn", cv.mark())

        DQS = None
        if "l0ffn" in kinds and "l1ffn" in kinds:
            l0_end = l0ffn_carve_end
            assert l0_end + 8194 >= kind_base + 68608, l0_end

            def _scr(rows, off, units, dt):
                vv = arena[0:rows, l0_end + off:l0_end + off + units]
                return vv if dt == BF16 else vv.bitcast(dt)

            DQS = {
                "shcol": _scr(128, 0, 2, U32),
                "zrow": _scr(4, 2, 5504, U32),
                "ztu": _scr(4, 5506, 5504, U32),
                "ztile": _scr(4, 11010, 5504, F32),
                "zrowd": _scr(11, 2, 1024, U32),
                "ztud": _scr(11, 1026, 8192, U32),
                "ztiled": _scr(11, 9218, 8192, F32),
                "g_nib": _scr(128, 2, 2752, U32),
                "g_zb": _scr(128, 2754, 2752, F32),
                "g_sb": _scr(128, 5506, 2752, F32),
                "g_nf": _scr(128, 8258, 2752, F32),
                "g_ob": _scr(128, 11010, 1376, BF16),
                "d_nib": _scr(128, 2, 2048, U32),
                "d_zb": _scr(128, 2050, 2048, F32),
                "d_sb": _scr(128, 4098, 2048, F32),
                "d_nf": _scr(128, 6146, 2048, F32),
                "d_ob": _scr(128, 8194, 1024, BF16),
            }
            assert l0_end + 17410 <= ARENA_N, l0_end

        semnames = ["wgc", "wcc", "xl", "cw", "stq", "rtr", "bcm", "wtr", "wxm", "hps",
                    "mmc", "mmo", "kmm", "vmm", "qmm", "omm", "smm", "pcn",
                    "bcs", "ccd", "vmc", "swp", "sto", "gmm", "umm", "dmm",
                    "dqw", "zw", "dql", "ccw", "vp", "sp",
                    "wsl0", "wsl1", "wsl2", "wsl3", "xls0", "xls1",
                    "vsl0", "vsl1", "kvl0", "kvl1"]
        for name in semnames:
            ctx.sems[name] = E(nc.semaphore(name))
        ctx.psems = {"v": ctx.sems["vp"], "s": ctx.sems["sp"]}

        ps = [E(nc.psum_tensor(f"ps{i}", [128, TT], F32)) for i in range(8)]

        stages = []
        nxi = 0
        l0_roles = iter(["gu", "dn"] if DQS is not None else [])
        for si, kind in enumerate(stage_list):
            if kind == "pro":
                stages.append(PRO(ctx, dram, S, (arena, kind_base), ps, CAST_WEIGHTS))
                continue
            if kind == "l0ffn":
                st_ = L0FFN(ctx, dram, S, KB["l0ffn"], ps, xbufs[nxi], xbufs[nxi + 1],
                            dq=next(l0_roles, None), scr=DQS)
                st_.wgc_target = WGC_IDX["wbT"]
                stages.append(st_)
            elif kind in ("attn0", "attn1"):
                st_ = ATTN(ctx, dram, S, KB["attn"], ps, xbufs[nxi], xbufs[nxi + 1],
                           l0=(kind == "attn0"))
                st_.wgc_target = WGC_IDX[f"{st_.pre}_woT"]
                stages.append(st_)
            else:
                stages.append(L1FFN(ctx, dram, S, KB["l1ffn"], ps, xbufs[nxi], xbufs[nxi + 1],
                                    dequant=False))
            nxi += 1

        block = E(nc.Block())

        def run_all(eng, meth):
            for si, st in enumerate(stages):
                ctx.si = si
                ctx.W(eng, "sto", 0)
                getattr(st, meth)(eng)

        @block.sync
        def _(sy0):
            run_all(Eng(sy0, ctx, "y"), "sync")

        @block.gpsimd
        def _(g0):
            run_all(Eng(g0, ctx, "g"), "gpsimd")

        @block.scalar
        def _(sc0):
            run_all(Eng(sc0, ctx, "s", ctx.sems["sp"]), "scalar")

        @block.vector
        def _(v0):
            run_all(Eng(v0, ctx, "v", ctx.sems["vp"]), "vector")

        @block.tensor
        def _(t0):
            run_all(Eng(t0, ctx, "t"), "tensor")
    return nc


def build_module(stage_list=STAGELIST, first_in=True, last_out=True):
    marks = {}
    nc = None
    for _pass in range(2):
        cnt = {}
        nc = _build(marks, cnt, stage_list, first_in, last_out)
        for (si, name), v in cnt.items():
            marks[f"__b__{si}__{name}"] = v
    return nc


class L0FFN:
    def __init__(self, ctx, dram, S, b, ps, xin, xout, dq=None, scr=None):
        self.c = ctx
        self.d = dram
        self.S = S
        self.b = b
        self.ps = ps
        self.xin = xin
        self.xout = xout
        self.dq = dq
        self.scr = scr
        self._dqi = 0
        if dq == "gu":
            self.dq_tiles = [("gu", g, jt, 1376) for g in range(4) for jt in range(16)]
        elif dq == "dn":
            self.dq_tiles = [("dn", g, jt, 1024) for g in range(11) for jt in range(4)]
        else:
            self.dq_tiles = []

    def _dq_step(self, v, n):
        if not self.dq:
            return
        c, scr = self.c, self.scr
        pfx = "g_" if self.dq == "gu" else "d_"
        for _ in range(n):
            i = self._dqi
            if i >= len(self.dq_tiles):
                return
            self._dqi += 1
            w = self.dq_tiles[i][3]
            nib = scr[pfx + "nib"][:, 0:w]
            nf = scr[pfx + "nf"][:, 0:w]
            c.W(v, "dql", 48 * (i + 1))
            if i >= 1:
                c.W(v, "dqw", 16 * i)
            v.tensor_scalar(out=nib, in0=nib, scalar1=scr["shcol"][:, 0:1],
                            scalar2=15, op0=ALU.logical_shift_right,
                            op1=ALU.bitwise_and)
            v.tensor_copy(nf, nib)
            v.tensor_sub(nf, nf, scr[pfx + "zb"][:, 0:w])
            v.tensor_mul(scr[pfx + "ob"][:, 0:w], nf, scr[pfx + "sb"][:, 0:w])
            v.mark(f"dq:{i}")

    def sync(self, sy):
        c, S, d = self.c, self.S, self.d
        s = 0
        for tt in range(NTT):
            if tt > 0:
                sy.waitm("v:ep:0:3:7")
                c.W(sy, "stq", KC * tt)
            c.I(sy.dma_start(
                out=self.b["xbuf"].rearrange("p (cc t) -> p cc t", cc=KC),
                in_=self.xin.rearrange("(cc p) t -> p cc t", p=128)[:, :, tt * TT:(tt + 1) * TT],
            ), "xl", 16)
            for og in range(4):
                for kc in range(KC):
                    if s >= 4:
                        c.W(sy, "mmc", s - 3)
                    if s == 0:
                        sy.wait_ge(c.sems["wgc"], self.wgc_target)
                    c.I(sy.dma_start(
                        out=S["wslab"][s % 4],
                        in_=d["wbT_bf"][kc * 128:(kc + 1) * 128, og * 1024:(og + 1) * 1024],
                    ), f"wsl{s % 4}", 16)
                    s += 1
                go = tt * 4 + og
                sy.waitm(f"v:ep:{tt}:{og}:7")
                dst = self.xout[og * 1024:(og + 1) * 1024, tt * TT:(tt + 1) * TT]
                c.I(sy.dma_start(
                    out=dst.rearrange("(J p) t -> p J t", p=128),
                    in_=S["outst"].rearrange("p (J t) -> p J t", J=8),
                ), "sto", 16)
        c.W(sy, "sto", 16 * 8)

    def gpsimd(self, g):
        c, b, d = self.c, self.b, self.d
        c.I(g.dma_start(out=b["atb"], in_=d["laP"][:, :]), "cw", 16)
        c.I(g.dma_start(out=b["btb"], in_=d["lbT"][:, :]), "cw", 16)
        c.I(g.dma_start(out=b["rtb"], in_=d["rtP"][:, :]), "cw", 16)
        c.I(g.dma_start(out=b["bcolb"], in_=d["bcol"][:, :]), "cw", 16)
        c.I(g.dma_start(out=b["e8b"], in_=d["e8"][:, :]), "cw", 16)
        c.I(g.dma_start(out=self.S["idfb"], in_=d["idf"][:, :]), "cw", 16)
        if not self.dq:
            return
        scr = self.scr
        if self.dq == "gu":
            c.I(g.dma_start(out=scr["zrow"], in_=d["qz_gu"][:, :]), "cw", 16)
        else:
            c.I(g.dma_start(out=scr["zrowd"], in_=d["qz_dn"][:, :]), "cw", 16)
        c.I(g.dma_start(out=scr["shcol"], in_=d["shin"][:, 0:1],
                        allow_slow_non_contiguous=True), "cw", 16)
        if self.dq == "gu":
            nzw = 8
            for jt in range(8):
                g.waitm(f"v:zgu:{jt}")
                c.I(g.dma_start(out=d["zscr_gu"][:, jt * 2752:(jt + 1) * 2752],
                                in_=scr["ztile"][0:4, 0:2752]), "zw", 16)
        else:
            nzw = 1
            g.waitm("v:zdn")
            c.I(g.dma_start(out=d["zscr_dn"][:, :], in_=scr["ztiled"][0:11, :]), "zw", 16)
        pfx = "g_" if self.dq == "gu" else "d_"
        c.W(g, "zw", 16 * nzw)
        for i, (kind, gg, jt, w) in enumerate(self.dq_tiles):
            if i >= 1:
                c.W(g, "dqw", 16 * i)
            if kind == "gu":
                qsrc = d["qw_gu"][gg * 16:(gg + 1) * 16, jt * w:(jt + 1) * w]
                zsrc = d["zscr_gu"][gg:gg + 1, jt * w:(jt + 1) * w]
                ssrc = d["sc_gu"][gg:gg + 1, jt * w:(jt + 1) * w]
                dst = d["dq_gu"].ap()[gg * 128:(gg + 1) * 128, jt * w:(jt + 1) * w]
            else:
                qsrc = d["qw_dn"][gg * 16:(gg + 1) * 16, jt * w:(jt + 1) * w]
                zsrc = d["zscr_dn"][gg:gg + 1, jt * w:(jt + 1) * w]
                ssrc = d["sc_dn"][gg:gg + 1, jt * w:(jt + 1) * w]
                dst = d["dq_dn"].ap()[gg * 128:(gg + 1) * 128, jt * w:(jt + 1) * w]
            c.I(g.dma_start(out=scr[pfx + "nib"][:, 0:w],
                            in_=qsrc.unsqueeze(1).broadcast_to([16, 8, w])), "dql", 16)
            c.W(g, "zw", 16 * nzw)
            c.I(g.dma_start(out=scr[pfx + "zb"][:, 0:w],
                            in_=zsrc.broadcast_to([128, w])), "dql", 16)
            c.I(g.dma_start(out=scr[pfx + "sb"][:, 0:w],
                            in_=ssrc.broadcast_to([128, w])), "dql", 16)
            g.waitm(f"v:dq:{i}")
            c.I(g.dma_start(out=dst, in_=scr[pfx + "ob"][:, 0:w]), "dqw", 16)
        if self.dq == "dn":
            c.W(g, "dqw", 16 * len(self.dq_tiles))
            c.I(g.collective_compute(
                "AllGather", mybir.AluOpType.bypass,
                replica_groups=[[0, 1, 2, 3, 4, 5, 6, 7]],
                ins=[d["dq_dn"].ap().opt()], outs=[d["wdn"].ap().opt()],
            ), "ccd", 1)

    def scalar(self, sc):
        c, S, b = self.c, self.S, self.b
        cns = S["cns"]
        sc.waitm("v:cst")
        for tt in range(NTT):
            c.W(sc, "xl", 16 * (tt + 1))
            for kc in range(KC):
                if tt * KC + kc >= 2:
                    c.W(sc, "stq", tt * KC + kc - 1)
                sc.activation(S["sq"][kc % 2], b["xbuf"][:, kc * TT:(kc + 1) * TT],
                              AF.Square, bias=cns[:, C_ZERO:C_ZERO + 1],
                              scale=cns[:, C_ONE:C_ONE + 1])
                sc.mark(f"sq:{tt}:{kc}")
            sc.waitm(f"v:var:{tt}")
            sc.activation(S["stat"][:, 2 * TT:3 * TT], S["stat"][:, TT:2 * TT],
                          AF.Ln, bias=cns[0:1, C_EPS:C_EPS + 1],
                          scale=cns[0:1, C_ONE:C_ONE + 1])
            sc.activation(S["stat"][:, 2 * TT:3 * TT], S["stat"][:, 2 * TT:3 * TT],
                          AF.Exp, bias=cns[0:1, C_ZERO:C_ZERO + 1],
                          scale=cns[0:1, C_NHALF:C_NHALF + 1])
            sc.mark(f"rstd:{tt}")
            for tc in range(4):
                sc.waitm(f"v:gmx:{tt}:{tc}")
                sc.activation(b["gcol"][:, 4:5], b["gcol"][:, 2:3], AF.Exp,
                              bias=cns[:, C_ZERO:C_ZERO + 1],
                              scale=cns[:, C_ONE:C_ONE + 1])
                sc.mark(f"gex:{tt}:{tc}")

    def vector(self, v):
        c, S, b, ps = self.c, self.S, self.b, self.ps
        cns = S["cns"]
        psA = ps[0][0:1, :]
        psB = ps[1][0:1, :]
        pr = ps[2][:, 0:8]
        pmb, prbc = ps[3], ps[4]
        ptr = ps[5][0:8, 0:128]
        pwx = ps[6]
        if c.si == 0:
            v.memset(cns[:, C_EPS:C_EPS + 1], EPS)
            v.memset(cns[:, C_NHALF:C_NHALF + 1], -0.5)
            v.memset(cns[:, C_INVD:C_INVD + 1], 1.0 / D)
            v.memset(cns[:, C_ZERO:C_ZERO + 1], 0.0)
            v.memset(cns[:, C_ONE:C_ONE + 1], 1.0)
            v.memset(cns[:, C_ISQ:C_ISQ + 1], 1.0 / float(np.sqrt(128.0)))
            v.memset(S["onesf"], 1.0)
            v.memset(S["ones1"], 1.0)
            v.memset(S["onbf"], 1.0)
        v.mark("cst")
        if self.dq:
            scr = self.scr
            c.W(v, "cw", 16 * 8)
            if self.dq == "gu":
                for jt in range(8):
                    if jt > 0:
                        c.W(v, "zw", 16 * jt)
                    for s8 in range(8):
                        v.tensor_scalar(
                            out=scr["ztu"][0:4, 0:2752].rearrange(
                                "p (a e) -> p a e", e=8)[:, :, s8],
                            in0=scr["zrow"][0:4, jt * 344:(jt + 1) * 344],
                            scalar1=4 * s8, scalar2=15,
                            op0=ALU.logical_shift_right, op1=ALU.bitwise_and)
                    v.tensor_scalar(out=scr["ztile"][0:4, 0:2752],
                                    in0=scr["ztu"][0:4, 0:2752],
                                    scalar1=1, scalar2=None, op0=ALU.add)
                    v.mark(f"zgu:{jt}")
            else:
                for s8 in range(8):
                    v.tensor_scalar(
                        out=scr["ztud"][0:11, 0:4096].rearrange(
                            "p (a e) -> p a e", e=8)[:, :, s8],
                        in0=scr["zrowd"][0:11, 0:512],
                        scalar1=4 * s8, scalar2=15,
                        op0=ALU.logical_shift_right, op1=ALU.bitwise_and)
                v.tensor_scalar(out=scr["ztiled"][0:11, :],
                                in0=scr["ztud"][0:11, 0:4096],
                                scalar1=1, scalar2=None, op0=ALU.add)
                v.mark("zdn")
        mb, rb = S["mbs"][0], S["rbs"][0]
        for tt in range(NTT):
            c.W(v, "stq", KC * (tt + 1))
            v.tensor_scalar_mul(S["stat"][:, 0:TT], psA, cns[0:1, C_INVD:C_INVD + 1])
            v.tensor_scalar_mul(S["stat"][:, TT:2 * TT], psB, cns[0:1, C_INVD:C_INVD + 1])
            v.tensor_mul(S["tmpf"][0:1, :], S["stat"][:, 0:TT], S["stat"][:, 0:TT])
            v.tensor_sub(S["stat"][:, TT:2 * TT], S["stat"][:, TT:2 * TT], S["tmpf"][0:1, :])
            v.mark(f"var:{tt}")
            c.W(v, "bcm", tt + 1)
            v.tensor_copy(mb, pmb[:])
            v.tensor_copy(rb, prbc[:])
            v.mark(f"bce:{tt}")
            for kc in range(KC):
                v.tensor_sub(S["tmpf"], b["xbuf"][:, kc * TT:(kc + 1) * TT], mb)
                v.tensor_mul(b["lnb"][:, kc * TT:(kc + 1) * TT], S["tmpf"], rb)
                v.mark(f"lnr:{tt}:{kc}")
            self._dq_step(v, 4)
            for tc in range(4):
                c.W(v, "rtr", tt * 4 + tc + 1)
                gcol, g8, w8 = b["gcol"], b["g8"], b["w8"]
                v.reduce_max(gcol[:, 0:1], pr, axis=AX)
                v.tensor_scalar(out=g8, in0=pr, scalar1=gcol[:, 0:1], scalar2=None,
                                op0=ALU.is_equal)
                w8s = w8[:, tc * 8:(tc + 1) * 8]
                v.scalar_tensor_tensor(out=w8s, in0=g8, scalar=-1e30, in1=pr,
                                       op0=ALU.mult, op1=ALU.add)
                v.reduce_max(gcol[:, 1:2], w8s, axis=AX)
                v.tensor_sub(gcol[:, 2:3], gcol[:, 1:2], gcol[:, 0:1])
                v.mark(f"gmx:{tt}:{tc}")
                v.waitm(f"s:gex:{tt}:{tc}")
                v.tensor_scalar_add(gcol[:, 5:6], gcol[:, 4:5], 1.0)
                v.reciprocal(gcol[:, 6:7], gcol[:, 5:6])
                v.tensor_mul(gcol[:, 7:8], gcol[:, 4:5], gcol[:, 6:7])
                v.tensor_scalar(out=g8, in0=g8, scalar1=gcol[:, 6:7], scalar2=None,
                                op0=ALU.mult)
                v.tensor_scalar(out=w8s, in0=pr, scalar1=gcol[:, 1:2], scalar2=gcol[:, 7:8],
                                op0=ALU.is_equal, op1=ALU.mult)
                v.tensor_add(w8s, w8s, g8)
                v.mark(f"wrd:{tt}:{tc}")
            for tc in range(4):
                c.W(v, "wtr", tt * 4 + tc + 1)
                v.tensor_copy(b["wfm"][:, tc * 128:(tc + 1) * 128], ptr)
                v.mark(f"wfe:{tt}:{tc}")
            c.W(v, "wxm", tt + 1)
            v.tensor_copy(b["wexp"], pwx[:])
            v.mark(f"wxe:{tt}")
            c.W(v, "hps", tt + 1)
            v.tensor_mul(b["hbf"], ps[7][:], b["wexp"])
            v.mark(f"hbd:{tt}")
            for og in range(4):
                go = tt * 4 + og
                c.W(v, "mmo", go * 8 + 8)
                if go >= 1:
                    c.W(v, "sto", 16 * go)
                for oc in range(8):
                    oco = og * 8 + oc
                    v.tensor_scalar(out=S["tmp2"], in0=ps[oc][:],
                                    scalar1=b["bcolb"][:, oco:oco + 1],
                                    scalar2=0.5, op0=ALU.add, op1=ALU.mult)
                    v.tensor_add(S["outst"][:, oc * TT:(oc + 1) * TT], S["tmp2"],
                                 b["xbuf"][:, oco * TT:(oco + 1) * TT])
                    v.mark(f"ep:{tt}:{og}:{oc}")
                self._dq_step(v, 6)
        self._dq_step(v, len(self.dq_tiles))

    def tensor(self, t):
        c, S, b, ps = self.c, self.S, self.b, self.ps
        psA = ps[0][0:1, :]
        psB = ps[1][0:1, :]
        pr = ps[2][:, 0:8]
        pmb, prbc = ps[3], ps[4]
        ptr = ps[5][0:8, 0:128]
        pwx = ps[6]
        c.W(t, "cw", 16 * 6)
        t.waitm("v:cst")
        s = 0
        for tt in range(NTT):
            c.W(t, "xl", 16 * (tt + 1))
            if tt > 0:
                t.waitm("v:ep:0:3:7")
            for kc in range(KC):
                t.matmul(psA, S["onesf"], b["xbuf"][:, kc * TT:(kc + 1) * TT],
                         start=(kc == 0), stop=(kc == KC - 1))
                t.waitm(f"s:sq:{tt}:{kc}")
                c.I(t.matmul(psB, S["onesf"], S["sq"][kc % 2],
                             start=(kc == 0), stop=(kc == KC - 1)), "stq", 1)
            t.waitm(f"s:rstd:{tt}")
            t.matmul(pmb[:], S["ones1"], S["stat"][:, 0:TT], start=True, stop=True)
            c.I(t.matmul(prbc[:], S["ones1"], S["stat"][:, 2 * TT:3 * TT],
                         start=True, stop=True), "bcm", 1)
            for tc in range(4):
                if tc > 0:
                    t.waitm(f"v:wrd:{tt}:{tc - 1}")
                elif tt > 0:
                    t.waitm("v:wrd:0:3")
                for kc in range(KC):
                    t.waitm(f"v:lnr:{tt}:{kc}")
                    mm = t.matmul(pr, b["lnb"][:, kc * TT + tc * 128:kc * TT + (tc + 1) * 128],
                                  b["rtb"][:, kc * 8:(kc + 1) * 8],
                                  start=(kc == 0), stop=(kc == KC - 1))
                c.I(mm, "rtr", 1)
            for kc in range(KC):
                t.waitm(f"v:lnr:{tt}:{kc}")
                mm = t.matmul(ps[7][:], b["atb"][:, kc * 128:(kc + 1) * 128],
                              b["lnb"][:, kc * TT:(kc + 1) * TT],
                              start=(kc == 0), stop=(kc == KC - 1))
            c.I(mm, "hps", 1)
            for tc in range(4):
                t.waitm(f"v:wrd:{tt}:{tc}")
                if tc > 0:
                    t.waitm(f"v:wfe:{tt}:{tc - 1}")
                elif tt > 0:
                    t.waitm("v:wfe:0:3")
                c.I(t.transpose(ptr, b["w8"][:, tc * 8:(tc + 1) * 8], S["idfb"]), "wtr", 1)
            t.waitm(f"v:wfe:{tt}:3")
            if tt > 0:
                t.waitm("v:wxe:0")
            c.I(t.matmul(pwx[:], b["e8b"], b["wfm"], start=True, stop=True), "wxm", 1)
            for og in range(4):
                go = tt * 4 + og
                if og == 0:
                    t.waitm(f"v:hbd:{tt}")
                for kc in range(KC):
                    c.W(t, f"wsl{s % 4}", 16 * (s // 4 + 1))
                    for jc in range(8):
                        if kc == 0 and og > 0:
                            t.waitm(f"v:ep:{tt}:{og - 1}:{jc}")
                        mm = t.matmul(
                            ps[jc][:],
                            S["wslab"][s % 4][:, jc * 128:(jc + 1) * 128],
                            b["lnb"][:, kc * TT:(kc + 1) * TT],
                            start=(kc == 0), stop=False,
                        )
                    c.I(mm, "mmc", 1)
                    s += 1
                t.waitm(f"v:hbd:{tt}")
                for jc in range(8):
                    oc = og * 8 + jc
                    c.I(t.matmul(ps[jc][:], b["btb"][:, oc * 128:(oc + 1) * 128], b["hbf"],
                                 start=False, stop=True), "mmo", 1)


class ATTN:
    def __init__(self, ctx, dram, S, b, ps, xin, xout, l0):
        self.c = ctx
        self.d = dram
        self.S = S
        self.b = b
        self.ps = ps
        self.xin = xin
        self.xout = xout
        self.l0 = l0
        self.pre = "a0" if l0 else "a1"
        self.SCHED = attn_g_sched()
        self.rope_phases = [("k", tt, 0) for tt in range(NTT)] + \
                           [("q", tq, og) for tq in range(NTT) for og in range(4)]
        self.rope_idx = {ph: i for i, ph in enumerate(self.rope_phases)}
        xp = []
        for tt in range(NTT):
            xp += [("st", tt), ("ln", tt)]
        for tq in range(NTT):
            xp += [("lq", tq), ("rs", tq)]
        self.xload_i = {}
        self.xload_rev = {}
        n = 0
        for pk in xp:
            for kc in range(KC):
                self.xload_i[(pk, kc)] = n
                self.xload_rev[n] = (pk, kc)
                n += 1
        self.smm_i = {}
        self.pcn_i = {}
        self.pcn_end = {}
        cs = cp = 0
        for (tq, h, pp, P, NKB) in self.SCHED:
            for kb in range(NKB):
                cs += 1
                self.smm_i[(P, kb)] = cs
            for kb in range(NKB):
                cp += 1
                self.pcn_i[(P, kb)] = cp
            self.pcn_end[P] = cp

    def _xgate(self, eng, gi):
        c = self.c
        if gi < 2:
            return
        pk, kc = self.xload_rev[gi - 2]
        kind, t = pk
        if kind == "st":
            eng.waitm(f"s:sq:{t}:{kc}")
            c.W(eng, "stq", t * KC + kc + 1)
        elif kind == "ln":
            eng.waitm(f"v:ln:{t}:{kc}")
        elif kind == "lq":
            eng.waitm(f"v:lq:{t}:{kc}")
        else:
            eng.waitm(f"v:ep:{t}:{kc}")

    def _xload(self, sy, pk, kc):
        c = self.c
        gi = self.xload_i[(pk, kc)]
        self._xgate(sy, gi)
        tt = pk[1]
        c.I(sy.dma_start(
            out=self.S["xst"][gi % 2],
            in_=self.xin.rearrange("(cc p) t -> p cc t", p=128)[:, kc, tt * TT:(tt + 1) * TT],
        ), f"xls{gi % 2}", 16)

    def sync(self, sy):
        c, S, b, d = self.c, self.S, self.b, self.d
        for tt in range(NTT):
            for kc in range(KC):
                self._xload(sy, ("st", tt), kc)
            for kc in range(KC):
                self._xload(sy, ("ln", tt), kc)
        for tt in range(NTT):
            for h in range(NKV):
                sy.waitm(f"v:krp:{tt}:{h}")
        for tt in range(NTT):
            for kvt in range(2):
                for tc in range(4):
                    sy.waitm(f"v:vev:{tt}:{kvt}:{tc}")
        kbuf = b["qkv"][:, 0:8 * 1024]
        vbuf = b["qkv"][:, 8 * 1024:16 * 1024]
        c.I(sy.dma_start(out=d["cc_in"].ap()[0:1024, :].rearrange("(h p) t -> p h t", p=128),
                         in_=kbuf.rearrange("p (h t) -> p h t", h=8)), "ccw", 16)
        c.I(sy.dma_start(out=d["cc_in"].ap()[1024:2048, :].rearrange("(cc p) m -> p cc m", p=128),
                         in_=vbuf.rearrange("p (cc m) -> p cc m", cc=8)), "ccw", 16)
        for tq in range(NTT):
            for kc in range(KC):
                self._xload(sy, ("lq", tq), kc)
            for h in range(NKV):
                ghi = tq * NKV + h
                if ghi == 0:
                    c.W(sy, "ccd", 1)
                if ghi >= 2:
                    Pdone = (ghi - 2) * 2 + 1
                    c.W(sy, "pcn", self.pcn_end[Pdone])
                bb = ghi % 2
                for half in range(2):
                    c.I(sy.dma_start(
                        out=b["ksb"][bb][:, half * 1024:(half + 1) * 1024],
                        in_=d["cc_out"].ap()[half * 2048 + h * 128:half * 2048 + (h + 1) * 128, :],
                    ), f"kvl{bb}", 16)
                    c.I(sy.dma_start(
                        out=b["vsb"][bb][:, half * 1024:(half + 1) * 1024].rearrange(
                            "p (cc m) -> p cc m", cc=8),
                        in_=d["cc_out"].ap()[half * 2048 + 1024:half * 2048 + 2048,
                                             h * 128:(h + 1) * 128].rearrange(
                            "(cc p) m -> p cc m", p=128),
                    ), f"kvl{bb}", 16)
            for og in range(4):
                for oc in range(8):
                    self._xload(sy, ("rs", tq), og * 8 + oc)
                go = tq * 4 + og
                sy.waitm(f"v:ep:{tq}:{og * 8 + 7}")
                dst = self.xout[og * 1024:(og + 1) * 1024, tq * TT:(tq + 1) * TT]
                c.I(sy.dma_start(
                    out=dst.rearrange("(J p) t -> p J t", p=128),
                    in_=S["outst"].rearrange("p (J t) -> p J t", J=8),
                ), "sto", 16)
        c.W(sy, "sto", 16 * 8)

    def gpsimd(self, g):
        c, S, b, d = self.c, self.S, self.b, self.d
        pre = self.pre
        nconst = 5 if self.l0 else 4
        c.I(g.dma_start(out=b["csb"], in_=d["csin"][:, :]), "cw", 16)
        c.I(g.dma_start(out=b["snb"], in_=d["snin"][:, :]), "cw", 16)
        c.I(g.dma_start(out=b["mdb"].rearrange("p (ss t) -> p ss t", ss=24),
                        in_=d["mdin"].rearrange("(ss p) t -> p ss t", p=128)), "cw", 16)
        c.I(g.dma_start(out=b["bcb"], in_=d["bcin"][:, :]), "cw", 16)
        if self.l0:
            c.I(g.dma_start(out=b["ksb_t"].rearrange("p (ss t) -> p ss t", ss=10),
                            in_=d["ksin"].rearrange("(ss p) t -> p ss t", p=128)), "cw", 16)
        self.nconst = nconst
        s = 0
        vs = 0

        g.wait_ge(c.sems["wgc"], self.wgc_target)

        def wload(kind, t_, og, kc):
            nonlocal s
            if s >= 4:
                c.W(g, "mmc", s - 3)
            if kind == "k":
                src = d[f"{pre}_wkT_bf"][kc * 128:(kc + 1) * 128, 0:1024]
            elif kind == "q":
                src = d[f"{pre}_wqT_bf"][kc * 128:(kc + 1) * 128, og * 1024:(og + 1) * 1024]
            else:
                src = d[f"{pre}_woT_bf"][kc * 128:(kc + 1) * 128, og * 1024:(og + 1) * 1024]
            c.I(g.dma_start(out=S["wslab"][s % 4], in_=src), f"wsl{s % 4}", 16)
            s += 1

        def vload(tt, kvt, kc):
            nonlocal vs
            if vs >= 2:
                c.W(g, "vmc", vs - 1)
            c.I(g.dma_start(out=S["vslab"][vs % 2],
                            in_=d[f"{pre}_wvT_bf"][kc * 128:(kc + 1) * 128,
                                                   kvt * 512:(kvt + 1) * 512]), f"vsl{vs % 2}", 16)
            vs += 1

        for tt in range(NTT):
            for kc in range(KC):
                wload("k", tt, 0, kc)
            for kvt in range(2):
                for kc in range(KC):
                    vload(tt, kvt, kc)
        c.W(g, "ccw", 32)
        c.I(g.collective_compute(
            "AllGather", mybir.AluOpType.bypass,
            replica_groups=[[0, 1], [2, 3], [4, 5], [6, 7]],
            ins=[d["cc_in"].ap().opt()],
            outs=[d["cc_out"].ap().opt()],
        ), "ccd", 1)
        if self.l0 and "dq_gu" in d:
            c.W(g, "dqw", 0)
            c.I(g.collective_compute(
                "AllGather", mybir.AluOpType.bypass,
                replica_groups=[[0, 1, 2, 3, 4, 5, 6, 7]],
                ins=[d["dq_gu"].ap().opt()], outs=[d["wgu"].ap().opt()],
            ), "ccd", 1)
        for tq in range(NTT):
            for og in range(4):
                for kc in range(KC):
                    wload("q", tq, og, kc)
            for og in range(4):
                for kc in range(KC):
                    wload("o", tq, og, kc)

    def scalar(self, sc):
        c, S, b = self.c, self.S, self.b
        cns = S["cns"]
        ps = self.ps
        sc.waitm("v:cst")

        def swap_phase(ph):
            ri = self.rope_idx[ph]
            kind, t_, og = ph
            sc.waitm(f"v:evp:{kind}:{t_}:{og}")
            if ri > 0:
                pk, pt, pog = self.rope_phases[ri - 1]
                sc.waitm(f"v:rope:{pk}:{pt}:{pog}")
            c.I(sc._e.dma_start(out=b["qswp"][0:64, :], in_=b["qpre"][64:128, :]), "swp", 16)
            c.I(sc._e.dma_start(out=b["qswp"][64:128, :], in_=b["qpre"][0:64, :]), "swp", 16)

        for tt in range(NTT):
            for kc in range(KC):
                gi = self.xload_i[(("st", tt), kc)]
                c.W(sc, f"xls{gi % 2}", 16 * (gi // 2 + 1))
                if tt * KC + kc >= 2:
                    c.W(sc, "stq", tt * KC + kc - 1)
                sc.activation(S["sq"][kc % 2], S["xst"][gi % 2],
                              AF.Square, bias=cns[:, C_ZERO:C_ZERO + 1],
                              scale=cns[:, C_ONE:C_ONE + 1])
                sc.mark(f"sq:{tt}:{kc}")
            sc.waitm(f"v:var:{tt}")
            sc.activation(S["stat"][:, 2 * TT:3 * TT], S["stat"][:, TT:2 * TT],
                          AF.Ln, bias=cns[0:1, C_EPS:C_EPS + 1],
                          scale=cns[0:1, C_ONE:C_ONE + 1])
            sc.activation(S["stat"][:, 2 * TT:3 * TT], S["stat"][:, 2 * TT:3 * TT],
                          AF.Exp, bias=cns[0:1, C_ZERO:C_ZERO + 1],
                          scale=cns[0:1, C_NHALF:C_NHALF + 1])
            sc.mark(f"rstd:{tt}")
            swap_phase(("k", tt, 0))
        for tq in range(NTT):
            for og in range(4):
                swap_phase(("q", tq, og))
            for (tq2, h, pp, P, NKB) in self.SCHED:
                if tq2 != tq:
                    continue
                for kb in range(NKB):
                    c.W(sc, "smm", self.smm_i[(P, kb)])
                    if self.l0 and kb <= 4:
                        sc.waitm(f"v:ks:{P}:{kb}")
                    if self.pcn_i[(P, kb)] > 2:
                        c.W(sc, "pcn", self.pcn_i[(P, kb)] - 2)
                    bc = b["bcb"][:, tq * 16 + kb:tq * 16 + kb + 1]
                    sc.activation(b["prb"][0][kb % 2], ps[0 + kb % 2][:], AF.Exp,
                                  bias=bc, scale=cns[:, C_ISQ:C_ISQ + 1])
                    sc.activation(b["prb"][1][kb % 2], ps[2 + kb % 2][:], AF.Exp,
                                  bias=bc, scale=cns[:, C_ISQ:C_ISQ + 1])
                    sc.mark(f"ex:{P}:{kb}")

    def vector(self, v):
        c, S, b, ps = self.c, self.S, self.b, self.ps
        cns = S["cns"]
        psA = ps[0][0:1, :]
        psB = ps[1][0:1, :]
        pmb, prbc = ps[2], ps[3]
        kbuf = b["qkv"][:, 0:8 * 1024]
        vbuf = b["qkv"][:, 8 * 1024:16 * 1024]
        qtile = b["qkv"][:, 0:16 * 1024]
        rc = S["stat"][:, 0:2 * TT]
        v.mark("cst")

        def rope_evac(kind, t_, og):
            ri = self.rope_idx[(kind, t_, og)]
            c.W(v, "swp", 32 * ri)
            for j in range(8):
                if kind == "k":
                    if j < 7:
                        c.W(v, "kmm", t_ * 7 + j + 1)
                    else:
                        c.W(v, "mmc", t_ * 32 + 32)
                else:
                    if j < 7:
                        c.W(v, "qmm", (t_ * 4 + og) * 7 + j + 1)
                    else:
                        c.W(v, "mmc", 64 + t_ * 256 + og * 32 + 32)
                v.tensor_copy(b["qpre"][:, j * TT:(j + 1) * TT], ps[j][:])
                v.mark(f"evc:{kind}:{t_}:{og}:{j}")
            v.mark(f"evp:{kind}:{t_}:{og}")

        def rope_tt(kind, t_, og, dst_of_head):
            ri = self.rope_idx[(kind, t_, og)]
            c.W(v, "swp", 32 * (ri + 1))
            cs_t = b["csb"][:, t_ * TT:(t_ + 1) * TT]
            sn_t = b["snb"][:, t_ * TT:(t_ + 1) * TT]
            for j in range(8):
                v.tensor_mul(S["tmpf"], b["qpre"][:, j * TT:(j + 1) * TT], cs_t)
                v.tensor_mul(S["tmp2"], b["qswp"][:, j * TT:(j + 1) * TT], sn_t)
                v.tensor_add(dst_of_head(j), S["tmpf"], S["tmp2"])
                if kind == "k":
                    v.mark(f"krp:{t_}:{j}")
                else:
                    v.mark(f"qrp:{t_}:{og * 8 + j}")
            v.mark(f"rope:{kind}:{t_}:{og}")

        for tt in range(NTT):
            c.W(v, "stq", KC * (tt + 1))
            v.tensor_scalar_mul(S["stat"][:, 0:TT], psA, cns[0:1, C_INVD:C_INVD + 1])
            v.tensor_scalar_mul(S["stat"][:, TT:2 * TT], psB, cns[0:1, C_INVD:C_INVD + 1])
            v.tensor_mul(S["tmpf"][0:1, :], S["stat"][:, 0:TT], S["stat"][:, 0:TT])
            v.tensor_sub(S["stat"][:, TT:2 * TT], S["stat"][:, TT:2 * TT], S["tmpf"][0:1, :])
            v.mark(f"var:{tt}")
            c.W(v, "bcm", tt + 1)
            v.tensor_copy(S["mbs"][tt], pmb[:])
            v.tensor_copy(S["rbs"][tt], prbc[:])
            v.mark(f"bce:{tt}")
            for kc in range(KC):
                gi = self.xload_i[(("ln", tt), kc)]
                c.W(v, f"xls{gi % 2}", 16 * (gi // 2 + 1))
                v.tensor_sub(S["tmpf"], S["xst"][gi % 2], S["mbs"][tt])
                v.tensor_mul(b["lnb"][:, kc * TT:(kc + 1) * TT], S["tmpf"], S["rbs"][tt])
                v.mark(f"ln:{tt}:{kc}")
            rope_evac("k", tt, 0)
            for kvt in range(2):
                for tc in range(4):
                    if tc < 3:
                        c.W(v, "vmm", (tt * 2 + kvt) * 3 + tc + 1)
                    else:
                        c.W(v, "vmc", (tt * 2 + kvt) * 32 + 32)
                    v.tensor_copy(
                        vbuf[:, (tt * 4 + tc) * 1024 + kvt * 512:
                             (tt * 4 + tc) * 1024 + (kvt + 1) * 512],
                        ps[tc][:])
                    v.mark(f"vev:{tt}:{kvt}:{tc}")
            rope_tt("k", tt, 0,
                    lambda j, tt=tt: kbuf[:, j * 1024 + tt * TT:j * 1024 + (tt + 1) * TT])
        for tq in range(NTT):
            for kc in range(KC):
                gi = self.xload_i[(("lq", tq), kc)]
                c.W(v, f"xls{gi % 2}", 16 * (gi // 2 + 1))
                v.tensor_sub(S["tmpf"], S["xst"][gi % 2], S["mbs"][tq])
                v.tensor_mul(b["lnb"][:, kc * TT:(kc + 1) * TT], S["tmpf"], S["rbs"][tq])
                v.mark(f"lq:{tq}:{kc}")
            for og in range(4):
                rope_evac("q", tq, og)
                rope_tt("q", tq, og,
                        lambda j, og=og: qtile[:, (og * 8 + j) * TT:(og * 8 + j + 1) * TT])
            for (tq2, h, pp, P, NKB) in self.SCHED:
                if tq2 != tq:
                    continue
                g0 = h * 4 + pp * 2
                for kb in range(NKB):
                    if self.l0 and kb <= 4:
                        c.W(v, "smm", self.smm_i[(P, kb)])
                        kst = b["ksb_t"][:, (tq * 5 + kb) * 512:(tq * 5 + kb + 1) * 512]
                        v.tensor_mul(ps[0 + kb % 2][:], ps[0 + kb % 2][:], kst)
                        v.tensor_mul(ps[2 + kb % 2][:], ps[2 + kb % 2][:], kst)
                        v.mark(f"ks:{P}:{kb}")
                    slot = diag_slot(tq, kb)
                    if slot is not None:
                        v.waitm(f"s:ex:{P}:{kb}")
                        md = b["mdb"][:, (tq * 12 + slot) * 512:(tq * 12 + slot + 1) * 512]
                        v.tensor_mul(b["prb"][0][kb % 2], b["prb"][0][kb % 2], md)
                        v.tensor_mul(b["prb"][1][kb % 2], b["prb"][1][kb % 2], md)
                        v.mark(f"mk:{P}:{kb}")
                c.W(v, "pcn", self.pcn_end[P])
                v.reciprocal(rc[:, 0:TT], ps[6][0:1, :])
                v.reciprocal(rc[:, TT:2 * TT], ps[7][0:1, :])
                v.mark(f"rcp:{P}")
                c.W(v, "bcs", 2 * P + 2)
                v.tensor_copy(b["pbc"][:, 0:TT], ps[6][:])
                v.tensor_mul(b["lnb"][:, g0 * TT:(g0 + 1) * TT], ps[4][:], b["pbc"][:, 0:TT])
                v.mark(f"att:{P}:0")
                v.tensor_copy(b["pbc"][:, TT:2 * TT], ps[7][:])
                v.tensor_mul(b["lnb"][:, (g0 + 1) * TT:(g0 + 2) * TT], ps[5][:],
                             b["pbc"][:, TT:2 * TT])
                v.mark(f"att:{P}:1")
            for og in range(4):
                go = tq * 4 + og
                if go >= 1:
                    c.W(v, "sto", 16 * go)
                for oc in range(8):
                    oco = og * 8 + oc
                    gi = self.xload_i[(("rs", tq), oco)]
                    if oc < 7:
                        c.W(v, "omm", (tq * 4 + og) * 7 + oc + 1)
                    else:
                        c.W(v, "mmc", 64 + tq * 256 + 128 + og * 32 + 32)
                    c.W(v, f"xls{gi % 2}", 16 * (gi // 2 + 1))
                    v.tensor_add(S["outst"][:, oc * TT:(oc + 1) * TT],
                                 ps[oc][:], S["xst"][gi % 2])
                    v.mark(f"ep:{tq}:{oco}")

    def tensor(self, t):
        c, S, b, ps = self.c, self.S, self.b, self.ps
        psA = ps[0][0:1, :]
        psB = ps[1][0:1, :]
        pmb, prbc = ps[2], ps[3]
        qtile = b["qkv"][:, 0:16 * 1024]
        rc = S["stat"][:, 0:2 * TT]
        c.W(t, "cw", 16 * (5 if self.l0 else 4))
        t.waitm("v:cst")
        bank_free = {j: None for j in range(8)}

        def bwait(j):
            if bank_free[j]:
                t.waitm(bank_free[j])

        s_ctr = [0]
        vs_ctr = [0]
        for tt in range(NTT):
            for kc in range(KC):
                gi = self.xload_i[(("st", tt), kc)]
                c.W(t, f"xls{gi % 2}", 16 * (gi // 2 + 1))
                if kc == 0:
                    bwait(0)
                    bwait(1)
                t.matmul(psA, S["onesf"], S["xst"][gi % 2],
                         start=(kc == 0), stop=(kc == KC - 1))
                t.waitm(f"s:sq:{tt}:{kc}")
                c.I(t.matmul(psB, S["onesf"], S["sq"][kc % 2],
                             start=(kc == 0), stop=(kc == KC - 1)), "stq", 1)
            bank_free[0] = f"v:var:{tt}"
            bank_free[1] = f"v:var:{tt}"
            t.waitm(f"s:rstd:{tt}")
            bwait(2)
            bwait(3)
            t.matmul(pmb[:], S["ones1"], S["stat"][:, 0:TT], start=True, stop=True)
            c.I(t.matmul(prbc[:], S["ones1"], S["stat"][:, 2 * TT:3 * TT],
                         start=True, stop=True), "bcm", 1)
            bank_free[2] = f"v:bce:{tt}"
            bank_free[3] = f"v:bce:{tt}"
            for kc in range(KC):
                s = s_ctr[0]
                c.W(t, f"wsl{s % 4}", 16 * (s // 4 + 1))
                t.waitm(f"v:ln:{tt}:{kc}")
                for j in range(8):
                    if kc == 0:
                        bwait(j)
                    mm = t.matmul(ps[j][:], S["wslab"][s % 4][:, j * 128:(j + 1) * 128],
                                  b["lnb"][:, kc * TT:(kc + 1) * TT],
                                  start=(kc == 0), stop=(kc == KC - 1))
                    if j == 7:
                        c.I(mm, "mmc", 1)
                    elif kc == KC - 1:
                        c.I(mm, "kmm", 1)
                s_ctr[0] += 1
            for j in range(8):
                bank_free[j] = f"v:evc:k:{tt}:0:{j}"
            for kvt in range(2):
                for kc in range(KC):
                    vs = vs_ctr[0]
                    c.W(t, f"vsl{vs % 2}", 16 * (vs // 2 + 1))
                    for tc in range(4):
                        if kc == 0:
                            bwait(tc)
                        mm = t.matmul(ps[tc][:],
                                      b["lnb"][:, kc * TT + tc * 128:kc * TT + (tc + 1) * 128],
                                      S["vslab"][vs % 2],
                                      start=(kc == 0), stop=(kc == KC - 1))
                        if tc == 3:
                            c.I(mm, "vmc", 1)
                        elif kc == KC - 1:
                            c.I(mm, "vmm", 1)
                    vs_ctr[0] += 1
                for tc in range(4):
                    bank_free[tc] = f"v:vev:{tt}:{kvt}:{tc}"
        for tq in range(NTT):
            for og in range(4):
                for kc in range(KC):
                    s = s_ctr[0]
                    c.W(t, f"wsl{s % 4}", 16 * (s // 4 + 1))
                    t.waitm(f"v:lq:{tq}:{kc}")
                    for j in range(8):
                        if kc == 0:
                            bwait(j)
                        mm = t.matmul(ps[j][:], S["wslab"][s % 4][:, j * 128:(j + 1) * 128],
                                      b["lnb"][:, kc * TT:(kc + 1) * TT],
                                      start=(kc == 0), stop=(kc == KC - 1))
                        if j == 7:
                            c.I(mm, "mmc", 1)
                        elif kc == KC - 1:
                            c.I(mm, "qmm", 1)
                    s_ctr[0] += 1
                for j in range(8):
                    bank_free[j] = f"v:evc:q:{tq}:{og}:{j}"
            for (tq2, h, pp, P, NKB) in self.SCHED:
                if tq2 != tq:
                    continue
                ghi = tq * NKV + h
                g0 = h * 4 + pp * 2
                if pp == 0:
                    c.W(t, f"kvl{ghi % 2}", 16 * 4 * (ghi // 2 + 1))
                t.waitm(f"v:qrp:{tq}:{g0}")
                t.waitm(f"v:qrp:{tq}:{g0 + 1}")

                def consume(kb, P=P, NKB=NKB, tq=tq, ghi=ghi):
                    slot = diag_slot(tq, kb)
                    if slot is not None:
                        t.waitm(f"v:mk:{P}:{kb}")
                    else:
                        t.waitm(f"s:ex:{P}:{kb}")
                    st_ = (kb == 0)
                    sp_ = (kb == NKB - 1)
                    if st_:
                        bwait(6)
                        bwait(4)
                    t.matmul(ps[6][:], S["onbf"], b["prb"][0][kb % 2], start=st_, stop=sp_)
                    t.matmul(ps[4][:], b["vsb"][ghi % 2][:, kb * 128:(kb + 1) * 128],
                             b["prb"][0][kb % 2], start=st_, stop=sp_)
                    if st_:
                        bwait(7)
                        bwait(5)
                    t.matmul(ps[7][:], S["onbf"], b["prb"][1][kb % 2], start=st_, stop=sp_)
                    c.I(t.matmul(ps[5][:], b["vsb"][ghi % 2][:, kb * 128:(kb + 1) * 128],
                                 b["prb"][1][kb % 2], start=st_, stop=sp_), "pcn", 1)

                for kb in range(NKB):
                    bwait(0 + kb % 2)
                    bwait(2 + kb % 2)
                    t.matmul(ps[0 + kb % 2][:], b["ksb"][ghi % 2][:, kb * 128:(kb + 1) * 128],
                             qtile[:, g0 * TT:(g0 + 1) * TT], start=True, stop=True)
                    c.I(t.matmul(ps[2 + kb % 2][:],
                                 b["ksb"][ghi % 2][:, kb * 128:(kb + 1) * 128],
                                 qtile[:, (g0 + 1) * TT:(g0 + 2) * TT], start=True,
                                 stop=True), "smm", 1)
                    bank_free[0 + kb % 2] = f"s:ex:{P}:{kb}"
                    bank_free[2 + kb % 2] = f"s:ex:{P}:{kb}"
                    if kb >= 1:
                        consume(kb - 1)
                consume(NKB - 1)
                t.waitm(f"v:rcp:{P}")
                c.I(t.matmul(ps[6][:], S["ones1"], rc[:, 0:TT],
                             start=True, stop=True), "bcs", 1)
                c.I(t.matmul(ps[7][:], S["ones1"], rc[:, TT:2 * TT],
                             start=True, stop=True), "bcs", 1)
                bank_free[4] = f"v:att:{P}:0"
                bank_free[5] = f"v:att:{P}:1"
                bank_free[6] = f"v:att:{P}:0"
                bank_free[7] = f"v:att:{P}:1"
            for og in range(4):
                for kc in range(KC):
                    s = s_ctr[0]
                    c.W(t, f"wsl{s % 4}", 16 * (s // 4 + 1))
                    if og == 0:
                        Pc = (tq * NKV + kc // 4) * 2 + (kc % 4) // 2
                        t.waitm(f"v:att:{Pc}:{kc % 2}")
                    for j in range(8):
                        if kc == 0:
                            if og == 0:
                                bwait(j)
                            else:
                                t.waitm(f"v:ep:{tq}:{(og - 1) * 8 + j}")
                        mm = t.matmul(ps[j][:], S["wslab"][s % 4][:, j * 128:(j + 1) * 128],
                                      b["lnb"][:, kc * TT:(kc + 1) * TT],
                                      start=(kc == 0), stop=(kc == KC - 1))
                        if j == 7:
                            c.I(mm, "mmc", 1)
                        elif kc == KC - 1:
                            c.I(mm, "omm", 1)
                    s_ctr[0] += 1
                for j in range(8):
                    bank_free[j] = f"v:ep:{tq}:{og * 8 + j}"


class L1FFN:
    def __init__(self, ctx, dram, S, b, ps, xin, xout, dequant):
        self.c = ctx
        self.d = dram
        self.S = S
        self.b = b
        self.ps = ps
        self.xin = xin
        self.xout = xout
        self.dequant = dequant
        self.slabs = []
        for tt in range(NTT):
            for (grp, njc) in GU_GRPS:
                for kind in ("g", "u"):
                    for kc in range(KC):
                        self.slabs.append((tt, grp, kind, kc, njc))
            for og in range(4):
                for fc in range(DN_F):
                    self.slabs.append((tt, og, "d", fc, 8))
        self.dq_tiles = [("gu", g, jt, 2752) for g in range(4) for jt in range(8)] + \
                        [("dn", g, jt, 2048) for g in range(11) for jt in range(2)]
        xp = []
        for tt in range(NTT):
            xp += [("st", tt), ("ln", tt), ("rs", tt)]
        self.xload_i = {}
        self.xload_rev = {}
        n = 0
        for pk in xp:
            for kc in range(KC):
                self.xload_i[(pk, kc)] = n
                self.xload_rev[n] = (pk, kc)
                n += 1
        self.pos = {}
        self.gmm_c = {}
        self.umm_c = {}
        self.dmm_c = {}
        cg = cu = cd = 0
        for i, (a, bb, kind, cc, njc) in enumerate(self.slabs):
            self.pos[(a, bb, kind, cc)] = i
            if kind == "g" and cc == KC - 1:
                for j in range(njc - 1):
                    cg += 1
                    self.gmm_c[(a, bb, j)] = cg
            if kind == "u" and cc == KC - 1:
                for j in range(njc - 1):
                    cu += 1
                    self.umm_c[(a, bb, j)] = cu
            if kind == "d" and cc == DN_F - 1:
                for j in range(7):
                    cd += 1
                    self.dmm_c[(a, bb, j)] = cd
        hbuf = None

    def _views(self):
        hbuf = self.b["hbuf"]
        return {
            "nibu": hbuf[:, 0:5504].bitcast(U32),
            "zb": hbuf[:, 5504:11008].bitcast(F32),
            "sb": hbuf[:, 11008:16512].bitcast(F32),
            "nf": hbuf[:, 16512:22016].bitcast(F32),
            "ob": hbuf[:, 22016:24768],
            "shtile": hbuf[:, 24768:30272].bitcast(U32),
            "m15": hbuf[:, 30272:35776].bitcast(U32),
            "ztu": self.b["lnb"][:, 0:8192].bitcast(U32),
            "ztile": self.S["outst"][0:11, 0:DN_J],
        }

    def _xgate(self, eng, gi):
        c = self.c
        if gi < 2:
            return
        pk, kc = self.xload_rev[gi - 2]
        kind, t_ = pk
        if kind == "st":
            eng.waitm(f"s:sq:{t_}:{kc}")
            c.W(eng, "stq", t_ * KC + kc + 1)
        elif kind == "ln":
            eng.waitm(f"v:ln:{t_}:{kc}")
        else:
            eng.waitm(f"v:ep:{t_}:{kc}")

    def _xload(self, sy, pk, kc):
        c = self.c
        gi = self.xload_i[(pk, kc)]
        self._xgate(sy, gi)
        tt = pk[1]
        c.I(sy.dma_start(
            out=self.S["xst"][gi % 2],
            in_=self.xin.rearrange("(cc p) t -> p cc t", p=128)[:, kc, tt * TT:(tt + 1) * TT],
        ), f"xls{gi % 2}", 16)

    def sync(self, sy):
        c, S = self.c, self.S
        for tt in range(NTT):
            for kc in range(KC):
                self._xload(sy, ("st", tt), kc)
            for kc in range(KC):
                self._xload(sy, ("ln", tt), kc)
            for og in range(4):
                for oc in range(8):
                    self._xload(sy, ("rs", tt), og * 8 + oc)
                sy.waitm(f"v:ep:{tt}:{og * 8 + 7}")
                dst = self.xout[og * 1024:(og + 1) * 1024, tt * TT:(tt + 1) * TT]
                c.I(sy.dma_start(
                    out=dst.rearrange("(J p) t -> p J t", p=128),
                    in_=S["outst"].rearrange("p (J t) -> p J t", J=8),
                ), "sto", 16)
        c.W(sy, "sto", 16 * 8)

    def gpsimd(self, g):
        c, S, b, d = self.c, self.S, self.b, self.d
        V = self._views()
        if self.dequant:
            c.I(g.dma_start(out=b["zrow"], in_=d["qz_gu"][:, :]), "cw", 16)
            c.I(g.dma_start(out=b["zrowd"], in_=d["qz_dn"][:, :]), "cw", 16)
            c.I(g.dma_start(out=V["shtile"], in_=d["shin"][:, :]), "cw", 16)
            for jt in range(8):
                g.waitm(f"v:zgu:{jt}")
                c.I(g.dma_start(out=d["zscr_gu"][:, jt * 2752:(jt + 1) * 2752],
                                in_=V["ztile"][0:4, 0:2752]), "zw", 16)
            g.waitm("v:zdn")
            c.I(g.dma_start(out=d["zscr_dn"][:, :], in_=V["ztile"][0:11, :]), "zw", 16)
            for i, (kind, gg, jt, w) in enumerate(self.dq_tiles):
                if i >= 1:
                    c.W(g, "dqw", 16 * i)
                if kind == "gu":
                    qsrc = d["qw_gu"][gg * 16:(gg + 1) * 16, jt * w:(jt + 1) * w]
                    zsrc = d["zscr_gu"][gg:gg + 1, jt * w:(jt + 1) * w]
                    ssrc = d["sc_gu"][gg:gg + 1, jt * w:(jt + 1) * w]
                    dst = d["dq_gu"].ap()[gg * 128:(gg + 1) * 128, jt * w:(jt + 1) * w]
                else:
                    qsrc = d["qw_dn"][gg * 16:(gg + 1) * 16, jt * w:(jt + 1) * w]
                    zsrc = d["zscr_dn"][gg:gg + 1, jt * w:(jt + 1) * w]
                    ssrc = d["sc_dn"][gg:gg + 1, jt * w:(jt + 1) * w]
                    dst = d["dq_dn"].ap()[gg * 128:(gg + 1) * 128, jt * w:(jt + 1) * w]
                c.I(g.dma_start(out=V["nibu"][:, 0:w],
                                in_=qsrc.unsqueeze(1).broadcast_to([16, 8, w])), "dql", 16)
                c.W(g, "zw", 16 * 9)
                c.I(g.dma_start(out=V["zb"][:, 0:w], in_=zsrc.broadcast_to([128, w])), "dql", 16)
                c.I(g.dma_start(out=V["sb"][:, 0:w], in_=ssrc.broadcast_to([128, w])), "dql", 16)
                g.waitm(f"v:dq:{i}")
                c.I(g.dma_start(out=dst, in_=V["ob"][:, 0:w]), "dqw", 16)
            c.W(g, "dqw", 16 * len(self.dq_tiles))
            c.I(g.collective_compute(
                "AllGather", mybir.AluOpType.bypass,
                replica_groups=[[0, 1, 2, 3, 4, 5, 6, 7]],
                ins=[d["dq_gu"].ap().opt()], outs=[d["wgu"].ap().opt()],
            ), "ccd", 1)
            c.I(g.collective_compute(
                "AllGather", mybir.AluOpType.bypass,
                replica_groups=[[0, 1, 2, 3, 4, 5, 6, 7]],
                ins=[d["dq_dn"].ap().opt()], outs=[d["wdn"].ap().opt()],
            ), "ccd", 1)
            c.W(g, "ccd", 2)
        else:
            c.W(g, "ccd", 0)
        s = 0
        for (a, bb, kind, cc, njc) in self.slabs:
            if s >= 4:
                c.W(g, "mmc", s - 3)
            if kind == "g":
                src = d["wgu"].ap()[cc * 128:(cc + 1) * 128,
                                    bb * 1024:bb * 1024 + njc * 128]
            elif kind == "u":
                src = d["wgu"].ap()[cc * 128:(cc + 1) * 128,
                                    11008 + bb * 1024:11008 + bb * 1024 + njc * 128]
            else:
                src = d["wdn"].ap()[cc * 128:(cc + 1) * 128,
                                    bb * 1024:(bb + 1) * 1024]
            c.I(g.dma_start(out=S["wslab"][s % 4][:, 0:njc * 128], in_=src), f"wsl{s % 4}", 16)
            s += 1

    def scalar(self, sc):
        c, S, b, ps = self.c, self.S, self.b, self.ps
        cns = S["cns"]
        sc.waitm("v:cst")
        for tt in range(NTT):
            for kc in range(KC):
                gi = self.xload_i[(("st", tt), kc)]
                c.W(sc, f"xls{gi % 2}", 16 * (gi // 2 + 1))
                if tt * KC + kc >= 2:
                    c.W(sc, "stq", tt * KC + kc - 1)
                sc.activation(S["sq"][kc % 2], S["xst"][gi % 2],
                              AF.Square, bias=cns[:, C_ZERO:C_ZERO + 1],
                              scale=cns[:, C_ONE:C_ONE + 1])
                sc.mark(f"sq:{tt}:{kc}")
            sc.waitm(f"v:var:{tt}")
            sc.activation(S["stat"][:, 2 * TT:3 * TT], S["stat"][:, TT:2 * TT],
                          AF.Ln, bias=cns[0:1, C_EPS:C_EPS + 1],
                          scale=cns[0:1, C_ONE:C_ONE + 1])
            sc.activation(S["stat"][:, 2 * TT:3 * TT], S["stat"][:, 2 * TT:3 * TT],
                          AF.Exp, bias=cns[0:1, C_ZERO:C_ZERO + 1],
                          scale=cns[0:1, C_NHALF:C_NHALF + 1])
            sc.mark(f"rstd:{tt}")
            for (grp, njc) in GU_GRPS:
                for j in range(njc):
                    if j < njc - 1:
                        c.W(sc, "gmm", self.gmm_c[(tt, grp, j)])
                    else:
                        c.W(sc, "mmc", self.pos[(tt, grp, "g", KC - 1)] + 1)
                    sc.activation(b["gsil"][:, j * TT:(j + 1) * TT], ps[j][:], AF.Silu,
                                  bias=cns[:, C_ZERO:C_ZERO + 1],
                                  scale=cns[:, C_ONE:C_ONE + 1])
                    sc.mark(f"sil:{tt}:{grp}:{j}")

    def vector(self, v):
        c, S, b, ps = self.c, self.S, self.b, self.ps
        cns = S["cns"]
        V = self._views()
        psA = ps[0][0:1, :]
        psB = ps[1][0:1, :]
        pmb, prbc = ps[2], ps[3]
        v.mark("cst")
        if self.dequant:
            c.W(v, "cw", 48)
            for jt in range(8):
                if jt > 0:
                    c.W(v, "zw", 16 * jt)
                for s8 in range(8):
                    v.tensor_scalar(
                        out=V["ztu"][0:4, 0:2752].rearrange("p (a e) -> p a e", e=8)[:, :, s8],
                        in0=b["zrow"][0:4, jt * 344:(jt + 1) * 344],
                        scalar1=4 * s8, scalar2=15,
                        op0=ALU.logical_shift_right, op1=ALU.bitwise_and)
                v.tensor_scalar(out=V["ztile"][0:4, 0:2752], in0=V["ztu"][0:4, 0:2752],
                                scalar1=1, scalar2=None, op0=ALU.add)
                v.mark(f"zgu:{jt}")
            c.W(v, "zw", 16 * 8)
            for s8 in range(8):
                v.tensor_scalar(
                    out=V["ztu"][0:11, 0:4096].rearrange("p (a e) -> p a e", e=8)[:, :, s8],
                    in0=b["zrowd"][0:11, 0:512],
                    scalar1=4 * s8, scalar2=15,
                    op0=ALU.logical_shift_right, op1=ALU.bitwise_and)
            v.tensor_scalar(out=V["ztile"][0:11, :], in0=V["ztu"][0:11, 0:4096],
                            scalar1=1, scalar2=None, op0=ALU.add)
            v.mark("zdn")
            v.memset(V["m15"][:, :], 15)
            for i, (kind, gg, jt, w) in enumerate(self.dq_tiles):
                c.W(v, "dql", 48 * (i + 1))
                if i >= 1:
                    c.W(v, "dqw", 16 * i)
                v.tensor_tensor(out=V["nibu"][:, 0:w], in0=V["nibu"][:, 0:w],
                                in1=V["shtile"][:, 0:w], op=ALU.logical_shift_right)
                v.tensor_tensor(out=V["nibu"][:, 0:w], in0=V["nibu"][:, 0:w],
                                in1=V["m15"][:, 0:w], op=ALU.bitwise_and)
                v.tensor_copy(V["nf"][:, 0:w], V["nibu"][:, 0:w])
                v.tensor_sub(V["nf"][:, 0:w], V["nf"][:, 0:w], V["zb"][:, 0:w])
                v.tensor_mul(V["ob"][:, 0:w], V["nf"][:, 0:w], V["sb"][:, 0:w])
                v.mark(f"dq:{i}")
        for tt in range(NTT):
            c.W(v, "stq", KC * (tt + 1))
            v.tensor_scalar_mul(S["stat"][:, 0:TT], psA, cns[0:1, C_INVD:C_INVD + 1])
            v.tensor_scalar_mul(S["stat"][:, TT:2 * TT], psB, cns[0:1, C_INVD:C_INVD + 1])
            v.tensor_mul(S["tmpf"][0:1, :], S["stat"][:, 0:TT], S["stat"][:, 0:TT])
            v.tensor_sub(S["stat"][:, TT:2 * TT], S["stat"][:, TT:2 * TT], S["tmpf"][0:1, :])
            v.mark(f"var:{tt}")
            c.W(v, "bcm", tt + 1)
            v.tensor_copy(S["mbs"][tt], pmb[:])
            v.tensor_copy(S["rbs"][tt], prbc[:])
            v.mark(f"bce:{tt}")
            for kc in range(KC):
                gi = self.xload_i[(("ln", tt), kc)]
                c.W(v, f"xls{gi % 2}", 16 * (gi // 2 + 1))
                v.tensor_sub(S["tmpf"], S["xst"][gi % 2], S["mbs"][tt])
                v.tensor_mul(b["lnb"][:, kc * TT:(kc + 1) * TT], S["tmpf"], S["rbs"][tt])
                v.mark(f"ln:{tt}:{kc}")
            for (grp, njc) in GU_GRPS:
                for j in range(njc):
                    fc = grp * 8 + j
                    v.waitm(f"s:sil:{tt}:{grp}:{j}")
                    if j < njc - 1:
                        c.W(v, "umm", self.umm_c[(tt, grp, j)])
                    else:
                        c.W(v, "mmc", self.pos[(tt, grp, "u", KC - 1)] + 1)
                    v.tensor_mul(b["hbuf"][:, fc * TT:(fc + 1) * TT],
                                 b["gsil"][:, j * TT:(j + 1) * TT], ps[j][:])
                    v.mark(f"h:{tt}:{grp}:{j}")
            for og in range(4):
                go = tt * 4 + og
                if go >= 1:
                    c.W(v, "sto", 16 * go)
                for oc in range(8):
                    oco = og * 8 + oc
                    gi = self.xload_i[(("rs", tt), oco)]
                    if oc < 7:
                        c.W(v, "dmm", self.dmm_c[(tt, og, oc)])
                    else:
                        c.W(v, "mmc", self.pos[(tt, og, "d", DN_F - 1)] + 1)
                    c.W(v, f"xls{gi % 2}", 16 * (gi // 2 + 1))
                    v.scalar_tensor_tensor(out=S["outst"][:, oc * TT:(oc + 1) * TT],
                                           in0=ps[oc][:], scalar=0.5,
                                           in1=S["xst"][gi % 2],
                                           op0=ALU.mult, op1=ALU.add)
                    v.mark(f"ep:{tt}:{oco}")

    def tensor(self, t):
        c, S, b, ps = self.c, self.S, self.b, self.ps
        psA = ps[0][0:1, :]
        psB = ps[1][0:1, :]
        pmb, prbc = ps[2], ps[3]
        if self.dequant:
            c.W(t, "cw", 48)
        t.waitm("v:cst")
        bank_free = {j: None for j in range(8)}

        def bwait(j):
            if bank_free[j]:
                t.waitm(bank_free[j])

        s_ctr = [0]
        for tt in range(NTT):
            for kc in range(KC):
                gi = self.xload_i[(("st", tt), kc)]
                c.W(t, f"xls{gi % 2}", 16 * (gi // 2 + 1))
                if kc == 0:
                    bwait(0)
                    bwait(1)
                t.matmul(psA, S["onesf"], S["xst"][gi % 2],
                         start=(kc == 0), stop=(kc == KC - 1))
                t.waitm(f"s:sq:{tt}:{kc}")
                c.I(t.matmul(psB, S["onesf"], S["sq"][kc % 2],
                             start=(kc == 0), stop=(kc == KC - 1)), "stq", 1)
            bank_free[0] = f"v:var:{tt}"
            bank_free[1] = f"v:var:{tt}"
            t.waitm(f"s:rstd:{tt}")
            bwait(2)
            bwait(3)
            t.matmul(pmb[:], S["ones1"], S["stat"][:, 0:TT], start=True, stop=True)
            c.I(t.matmul(prbc[:], S["ones1"], S["stat"][:, 2 * TT:3 * TT],
                         start=True, stop=True), "bcm", 1)
            bank_free[2] = f"v:bce:{tt}"
            bank_free[3] = f"v:bce:{tt}"
            for (grp, njc) in GU_GRPS:
                for kc in range(KC):
                    s = s_ctr[0]
                    c.W(t, f"wsl{s % 4}", 16 * (s // 4 + 1))
                    t.waitm(f"v:ln:{tt}:{kc}")
                    for j in range(njc):
                        if kc == 0:
                            bwait(j)
                        mm = t.matmul(ps[j][:], S["wslab"][s % 4][:, j * 128:(j + 1) * 128],
                                      b["lnb"][:, kc * TT:(kc + 1) * TT],
                                      start=(kc == 0), stop=(kc == KC - 1))
                        if j == njc - 1:
                            c.I(mm, "mmc", 1)
                        elif kc == KC - 1:
                            c.I(mm, "gmm", 1)
                    s_ctr[0] += 1
                for kc in range(KC):
                    s = s_ctr[0]
                    c.W(t, f"wsl{s % 4}", 16 * (s // 4 + 1))
                    for j in range(njc):
                        if kc == 0:
                            t.waitm(f"s:sil:{tt}:{grp}:{j}")
                        mm = t.matmul(ps[j][:], S["wslab"][s % 4][:, j * 128:(j + 1) * 128],
                                      b["lnb"][:, kc * TT:(kc + 1) * TT],
                                      start=(kc == 0), stop=(kc == KC - 1))
                        if j == njc - 1:
                            c.I(mm, "mmc", 1)
                        elif kc == KC - 1:
                            c.I(mm, "umm", 1)
                    s_ctr[0] += 1
                for j in range(njc):
                    bank_free[j] = f"v:h:{tt}:{grp}:{j}"
            for og in range(4):
                for fc in range(DN_F):
                    s = s_ctr[0]
                    c.W(t, f"wsl{s % 4}", 16 * (s // 4 + 1))
                    if og == 0:
                        t.waitm(f"v:h:{tt}:{fc // 8}:{fc % 8}")
                    for j in range(8):
                        if fc == 0:
                            if og == 0:
                                bwait(j)
                            else:
                                t.waitm(f"v:ep:{tt}:{(og - 1) * 8 + j}")
                        mm = t.matmul(ps[j][:], S["wslab"][s % 4][:, j * 128:(j + 1) * 128],
                                      b["hbuf"][:, fc * TT:(fc + 1) * TT],
                                      start=(fc == 0), stop=(fc == DN_F - 1))
                        if j == 7:
                            c.I(mm, "mmc", 1)
                        elif fc == DN_F - 1:
                            c.I(mm, "dmm", 1)
                    s_ctr[0] += 1
                for j in range(8):
                    bank_free[j] = f"v:ep:{tt}:{og * 8 + j}"



def x_to_cores(x):
    out = []
    for cc in range(8):
        bb, h = cc // 2, cc % 2
        sl = np.asarray(x)[bb, h * 1024:(h + 1) * 1024, :]
        out.append(np.ascontiguousarray(sl.T))
    return out


def x_from_cores(shards):
    x = np.empty((4, 2048, D), np.float32)
    for cc in range(8):
        bb, h = cc // 2, cc % 2
        x[bb, h * 1024:(h + 1) * 1024, :] = shards[cc].T
    return x


def chunked_cols(a):
    dd, m = a.shape
    return np.ascontiguousarray(a.reshape(KC, 128, m).transpose(1, 0, 2).reshape(128, KC * m))


def rope_tables(q0):
    import ml_dtypes
    inv = 1.0 / (10000.0 ** (np.arange(0, 64, dtype=np.float64) / 64.0))
    pos = np.arange(q0, q0 + TOK, dtype=np.float64)
    ang = pos[None, :] * inv[:, None]
    cs = np.cos(ang)
    sn = np.sin(ang)
    cs2 = np.concatenate([cs, cs], axis=0)
    sn2 = np.concatenate([-sn, sn], axis=0)
    return (cs2.astype(ml_dtypes.bfloat16), sn2.astype(ml_dtypes.bfloat16))


def host_inputs(full, core):
    import ml_dtypes
    ins = {}
    e8 = np.zeros((8, 128), np.float32)
    for e in range(8):
        e8[e, e * 16:(e + 1) * 16] = 2.0
    r0 = core * 512
    ins["wbT_sh"] = np.ascontiguousarray(np.asarray(full["L0_ffn_W_base"]).T[r0:r0 + 512])
    ins["bcol"] = np.ascontiguousarray(np.asarray(full["L0_ffn_b_base"]).reshape(32, 128).T)
    ins["rtP"] = chunked_cols(np.ascontiguousarray(np.asarray(full["L0_ffn_W_router"]).T))
    ins["laP"] = chunked_cols(np.ascontiguousarray(
        np.asarray(full["L0_ffn_lora_A"]).reshape(128, D).T))
    ins["lbT"] = np.ascontiguousarray(
        np.asarray(full["L0_ffn_lora_B"]).transpose(0, 2, 1).reshape(128, D))
    ins["e8"] = e8
    ins["idf"] = np.eye(128, dtype=np.float32)
    q0 = (core % 2) * 1024
    cs2, sn2 = rope_tables(q0)
    ins["csin"], ins["snin"] = cs2, sn2
    md = np.zeros((2, 12, 128, 512), np.float32)
    bcols = np.full((2, 16), -20.0, np.float32)
    jj = np.arange(512)
    ii = np.arange(128)
    for tq in range(2):
        qbase = q0 + tq * 512
        for kb in range(16):
            if kb < NKBL[tq]:
                valid = (qbase + jj[None, :]) >= (kb * 128 + ii[:, None])
                if not valid.any():
                    bcols[tq, kb] = -1e9
                slot = diag_slot(tq, kb)
                if slot is not None:
                    md[tq, slot] = valid.astype(np.float32)
    ins["mdin"] = md.reshape(24 * 128, 512).astype(ml_dtypes.bfloat16)
    ins["bcin"] = np.broadcast_to(bcols.reshape(1, 32), (128, 32)).copy()
    ks = np.ones((2, 5, 128, 512), np.float32)
    kpos = np.arange(128)
    for tq in range(2):
        qbase = q0 + tq * 512
        qsel = (qbase + jj) >= 611
        for kb in range(5):
            kv = kb * 128 + kpos
            kscale = np.where(kv < 35, 0.5, np.where(kv < 611, 1.5, 1.0))
            ks[tq, kb] = np.where(qsel[None, :], kscale[:, None], 1.0)
    ins["ksin"] = ks.reshape(10 * 128, 512)
    ins["a0_wqT_sh"] = np.ascontiguousarray(np.asarray(full["L0_attn_wq"]).T[r0:r0 + 512].astype(np.float32))
    ins["a0_wkT_sh"] = np.ascontiguousarray(np.asarray(full["L0_attn_wk"]).T[r0:r0 + 512].astype(np.float32))
    ins["a0_wvT_sh"] = np.ascontiguousarray(np.asarray(full["L0_attn_wv"]).T[r0:r0 + 512].astype(np.float32))
    ins["a0_woT_sh"] = np.ascontiguousarray(np.asarray(full["L0_attn_wo"]).T[r0:r0 + 512].astype(np.float32))
    perm = np.empty(128, np.int64)
    perm[:64] = np.arange(0, 128, 2)
    perm[64:] = np.arange(1, 128, 2)

    def permcols(w):
        w = np.asarray(w)
        nh = w.shape[1] // 128
        return np.ascontiguousarray(
            w.reshape(w.shape[0], nh, 128)[:, :, perm].reshape(w.shape))
    ins["a1_wqT_sh"] = np.ascontiguousarray(permcols(full["L1_attn_Wq"])[r0:r0 + 512].astype(np.float32))
    ins["a1_wkT_sh"] = np.ascontiguousarray(permcols(full["L1_attn_Wk"])[r0:r0 + 512].astype(np.float32))
    ins["a1_wvT_sh"] = np.ascontiguousarray(np.asarray(full["L1_attn_Wv"])[r0:r0 + 512].astype(np.float32))
    ins["a1_woT_sh"] = np.ascontiguousarray(np.asarray(full["L1_attn_Wo"])[r0:r0 + 512].astype(np.float32))
    qw_gu = np.asarray(full["L1_ffn_qweight_gu"])
    qz_gu = np.asarray(full["L1_ffn_qzeros_gu"])
    sc_gu = np.asarray(full["L1_ffn_scales_gu"])
    qw_dn = np.asarray(full["L1_ffn_qweight_down"])
    qz_dn = np.asarray(full["L1_ffn_qzeros_down"])
    sc_dn = np.asarray(full["L1_ffn_scales_down"])
    g0 = core * 4
    d0 = core * 11
    qw_dn_pad = np.zeros((176 * 8, DN_J), np.uint32)
    qw_dn_pad[:qw_dn.shape[0]] = qw_dn
    qz_dn_pad = np.zeros((88, 512), np.uint32)
    qz_dn_pad[:qz_dn.shape[0]] = qz_dn
    sc_dn_pad = np.zeros((88, DN_J), np.float32)
    sc_dn_pad[:sc_dn.shape[0]] = sc_dn
    ins["qw_gu"] = np.ascontiguousarray(qw_gu[core * 64:(core + 1) * 64])
    ins["qz_gu"] = np.ascontiguousarray(qz_gu[g0:g0 + 4])
    ins["sc_gu"] = np.ascontiguousarray(sc_gu[g0:g0 + 4])
    ins["qw_dn"] = np.ascontiguousarray(qw_dn_pad[core * 176:(core + 1) * 176])
    ins["qz_dn"] = np.ascontiguousarray(qz_dn_pad[d0:d0 + 11])
    ins["sc_dn"] = np.ascontiguousarray(sc_dn_pad[d0:d0 + 11])
    ins["shin"] = np.broadcast_to(
        (((np.arange(128, dtype=np.uint32) % 8) * 4)[:, None]), (128, 2752)).copy()
    return ins


_NC = [None]


def kernel(**inputs):
    shards = x_to_cores(inputs["L0_attn_hidden_states"])
    if _NC[0] is None:
        _NC[0] = build_module()
    in_maps = [dict(host_inputs(inputs, cc), x_in=shards[cc]) for cc in range(8)]
    res = run_bass_kernel_spmd(_NC[0], in_maps, core_ids=list(range(8)))
    return x_from_cores([np.ascontiguousarray(res.results[cc]["x_out"]) for cc in range(8)])


class PRO:

    def __init__(self, ctx, dram, S, region, ps, cast_weights):
        self.c = ctx
        self.d = dram
        self.S = S
        self.ps = ps
        self.cw_list = cast_weights
        arena, kb0 = region
        o = [kb0]

        def take(rows, units):
            st = o[0]
            o[0] += units
            return arena[0:rows, st:st + units]

        self.nibu = take(128, 5504).bitcast(U32)
        self.zb = take(128, 5504).bitcast(F32)
        self.sb = take(128, 5504).bitcast(F32)
        self.nf = take(128, 5504).bitcast(F32)
        self.ob = take(128, 2752)
        self.shtile = take(128, 5504).bitcast(U32)
        self.m15 = take(128, 5504).bitcast(U32)
        self.ztu = take(128, 5504).bitcast(U32)
        self.ztile = take(11, 8192).bitcast(F32)
        self.zrow = take(4, 5504).bitcast(U32)
        self.zrowd = take(11, 1024).bitcast(U32)
        self.dq_tiles = [("gu", g, jt, 2752) for g in range(4) for jt in range(8)] + \
                        [("dn", g, jt, 2048) for g in range(11) for jt in range(2)]

    def sync(self, sy):
        pass

    def tensor(self, t):
        pass

    def gpsimd(self, g):
        c, d = self.c, self.d
        for i, (wn, wd) in enumerate(self.cw_list):
            c.I(g.dma_start(out=d[f"{wn}_cast"], in_=d[f"{wn}_sh"]), "wcc", 16)
        c.W(g, "wcc", 16 * len(self.cw_list))
        for i, (wn, wd) in enumerate(self.cw_list):
            c.I(g.collective_compute(
                "AllGather", mybir.AluOpType.bypass,
                replica_groups=[[0, 1, 2, 3, 4, 5, 6, 7]],
                ins=[d[f"{wn}_cast"].opt()], outs=[d[f"{wn}_bf"].opt()],
            ), "wgc", 1)

    def scalar(self, sc):
        return
        c, d = self.c, self.d
        if "dq_gu" not in d:
            return
        e = sc._e
        c.I(e.dma_start(out=self.zrow, in_=d["qz_gu"][:, :]), "cw", 16)
        c.I(e.dma_start(out=self.zrowd, in_=d["qz_dn"][:, :]), "cw", 16)
        c.I(e.dma_start(out=self.shtile, in_=d["shin"][:, :]), "cw", 16)
        for jt in range(8):
            sc.waitm(f"v:zgu:{jt}")
            c.I(e.dma_start(out=d["zscr_gu"][:, jt * 2752:(jt + 1) * 2752],
                            in_=self.ztile[0:4, 0:2752]), "zw", 16)
        sc.waitm("v:zdn")
        c.I(e.dma_start(out=d["zscr_dn"][:, :], in_=self.ztile[0:11, :]), "zw", 16)
        for i, (kind, gg, jt, w) in enumerate(self.dq_tiles):
            if i >= 1:
                c.W(sc, "dqw", 16 * i)
            if kind == "gu":
                qsrc = d["qw_gu"][gg * 16:(gg + 1) * 16, jt * w:(jt + 1) * w]
                zsrc = d["zscr_gu"][gg:gg + 1, jt * w:(jt + 1) * w]
                ssrc = d["sc_gu"][gg:gg + 1, jt * w:(jt + 1) * w]
                dst = d["dq_gu"].ap()[gg * 128:(gg + 1) * 128, jt * w:(jt + 1) * w]
            else:
                qsrc = d["qw_dn"][gg * 16:(gg + 1) * 16, jt * w:(jt + 1) * w]
                zsrc = d["zscr_dn"][gg:gg + 1, jt * w:(jt + 1) * w]
                ssrc = d["sc_dn"][gg:gg + 1, jt * w:(jt + 1) * w]
                dst = d["dq_dn"].ap()[gg * 128:(gg + 1) * 128, jt * w:(jt + 1) * w]
            c.I(e.dma_start(out=self.nibu[:, 0:w],
                            in_=qsrc.unsqueeze(1).broadcast_to([16, 8, w])), "dql", 16)
            c.W(sc, "zw", 16 * 9)
            c.I(e.dma_start(out=self.zb[:, 0:w], in_=zsrc.broadcast_to([128, w])), "dql", 16)
            c.I(e.dma_start(out=self.sb[:, 0:w], in_=ssrc.broadcast_to([128, w])), "dql", 16)
            sc.waitm(f"v:dq:{i}")
            c.I(e.dma_start(out=dst, in_=self.ob[:, 0:w]), "dqw", 16)

    def vector(self, v):
        c = self.c
        S = self.S
        cns = S["cns"]
        v.memset(cns[:, C_EPS:C_EPS + 1], EPS)
        v.memset(cns[:, C_NHALF:C_NHALF + 1], -0.5)
        v.memset(cns[:, C_INVD:C_INVD + 1], 1.0 / D)
        v.memset(cns[:, C_ZERO:C_ZERO + 1], 0.0)
        v.memset(cns[:, C_ONE:C_ONE + 1], 1.0)
        v.memset(cns[:, C_ISQ:C_ISQ + 1], 1.0 / float(np.sqrt(128.0)))
        v.memset(S["onesf"], 1.0)
        v.memset(S["ones1"], 1.0)
        v.memset(S["onbf"], 1.0)
        v.mark("cst")
        return
        c.W(v, "cw", 48)
        for jt in range(8):
            if jt > 0:
                c.W(v, "zw", 16 * jt)
            for s8 in range(8):
                v.tensor_scalar(
                    out=self.ztu[0:4, 0:2752].rearrange("p (a e) -> p a e", e=8)[:, :, s8],
                    in0=self.zrow[0:4, jt * 344:(jt + 1) * 344],
                    scalar1=4 * s8, scalar2=15,
                    op0=ALU.logical_shift_right, op1=ALU.bitwise_and)
            v.tensor_scalar(out=self.ztile[0:4, 0:2752], in0=self.ztu[0:4, 0:2752],
                            scalar1=1, scalar2=None, op0=ALU.add)
            v.mark(f"zgu:{jt}")
        c.W(v, "zw", 16 * 8)
        for half in range(2):
            for s8 in range(8):
                v.tensor_scalar(
                    out=self.ztu[0:11, 0:2048].rearrange("p (a e) -> p a e", e=8)[:, :, s8],
                    in0=self.zrowd[0:11, half * 256:(half + 1) * 256],
                    scalar1=4 * s8, scalar2=15,
                    op0=ALU.logical_shift_right, op1=ALU.bitwise_and)
            v.tensor_scalar(out=self.ztile[0:11, half * 2048:(half + 1) * 2048],
                            in0=self.ztu[0:11, 0:2048],
                            scalar1=1, scalar2=None, op0=ALU.add)
        v.mark("zdn")
        v.memset(self.m15[:, :], 15)
        for i, (kind, gg, jt, w) in enumerate(self.dq_tiles):
            c.W(v, "dql", 48 * (i + 1))
            if i >= 1:
                c.W(v, "dqw", 16 * i)
            v.tensor_tensor(out=self.nibu[:, 0:w], in0=self.nibu[:, 0:w],
                            in1=self.shtile[:, 0:w], op=ALU.logical_shift_right)
            v.tensor_tensor(out=self.nibu[:, 0:w], in0=self.nibu[:, 0:w],
                            in1=self.m15[:, 0:w], op=ALU.bitwise_and)
            v.tensor_copy(self.nf[:, 0:w], self.nibu[:, 0:w])
            v.tensor_sub(self.nf[:, 0:w], self.nf[:, 0:w], self.zb[:, 0:w])
            v.tensor_mul(self.ob[:, 0:w], self.nf[:, 0:w], self.sb[:, 0:w])
            v.mark(f"dq:{i}")
